# version 11
# baseline (speedup 1.0000x reference)
import numpy as np
from contextlib import ExitStack

import concourse.bass as bass
import concourse.mybir as mybir
from concourse.bass_utils import run_bass_kernel_spmd

F32 = mybir.dt.float32
BF16 = mybir.dt.bfloat16
F8 = mybir.dt.float8e4
AF = mybir.ActivationFunctionType
ALU = mybir.AluOpType

NCORES = 8
D = 2048
KC = 16
T = 1024
TH = 512
FFD = 8192
EPS_RMS = 1.1920929e-07
EPS_LN = 1e-5
NBUF = 2


class Ctr:
    def __init__(self, stack, nc, name):
        self.sem = stack.enter_context(nc.semaphore(name))
        self.n = 0

    def inc(self, instr, k=1):
        instr.then_inc(self.sem, k)
        self.n += k
        return self.n


def build(kind):
    nc = bass.Bass("TRN2", target_bir_lowering=False,
                   detect_race_conditions=(kind != "F"))

    xT = nc.dram_tensor("xT", [D, T], F32, kind="ExternalInput").ap()
    wqkvT = nc.dram_tensor("wqkvT", [D, 3 * D], F32, kind="ExternalInput").ap()
    cT = nc.dram_tensor("cT", [128, KC], F32, kind="ExternalInput").ap()
    wmod = nc.dram_tensor("wmod", [D, 6 * D], F32, kind="ExternalInput").ap()
    if kind != "A":
        x2pd = nc.dram_tensor("x2pd", [D, T], F32, kind="Internal").ap()
        wswT = nc.dram_tensor("wswT", [D, 2 * FFD], F32, kind="ExternalInput").ap()
        woutT = nc.dram_tensor("woutT", [FFD, D], F32, kind="ExternalInput").ap()
        wg = nc.dram_tensor("wg", [2, D, FFD], F32, kind="ExternalInput").ap()
        wlin = nc.dram_tensor("wlin", [FFD, D], F32, kind="ExternalInput").ap()
        out = nc.dram_tensor("out", [D, T], F32, kind="ExternalOutput").ap()
    if kind == "B":
        tot8 = nc.dram_tensor("tot8", [8, 128, 32], F32, kind="ExternalInput").ap()
        mask8 = nc.dram_tensor("mask8", [128, 8], F32, kind="ExternalInput").ap()
    if kind == "F":
        xpre = nc.dram_tensor("xpre", [D, 4 * T], F32, kind="ExternalInput").ap()
        maskp = nc.dram_tensor("maskp", [128, 8], F32, kind="ExternalInput").ap()
        msib = nc.dram_tensor("msib", [128, 1], F32, kind="ExternalInput").ap()
        shx = nc.dram_tensor("shx", [2, 128, 68], F32, kind="Internal",
                             addr_space="Shared").ap()
    if kind == "A":
        tot = nc.dram_tensor("tot", [128, 32], F32, kind="ExternalOutput").ap()

    NMOD = 8 if kind == "A" else 24

    with ExitStack() as stx:
        e = stx.enter_context
        BUF1 = e(nc.sbuf_tensor("BUF1", [128, KC, T], F32))
        BUF2 = e(nc.sbuf_tensor("BUF2", [128, KC, T], F32))
        BUF3 = e(nc.sbuf_tensor("BUF3", [128, KC, T], F32))
        WST = e(nc.sbuf_tensor("WST", [128, NBUF, 512], F32))
        WHL = e(nc.sbuf_tensor("WHL", [128, NBUF, 512], F32))
        TMP = e(nc.sbuf_tensor("TMP", [128, 2, TH], F32))
        MODC = e(nc.sbuf_tensor("MODC", [128, 96], F32))
        SC = e(nc.sbuf_tensor("SC", [128, KC], F32))
        SCs = e(nc.sbuf_tensor("SCs", [128, KC], F32))
        SCh = e(nc.sbuf_tensor("SCh", [128, KC], BF16))
        SCl = e(nc.sbuf_tensor("SCl", [128, KC], BF16))
        ONESC = e(nc.sbuf_tensor("ONESC", [128, 1], F32))
        ONESR = e(nc.sbuf_tensor("ONESR", [1, 128], F32))
        EPSC = e(nc.sbuf_tensor("EPSC", [1, 3], F32))
        EPSB = e(nc.sbuf_tensor("EPSB", [128, 1], F32))
        if kind != "A":
            MSCR = e(nc.sbuf_tensor("MSCR", [128, 32], F32))
            CAR = e(nc.sbuf_tensor("CAR", [128, 32], F32))
        if kind == "B":
            TOT8S = e(nc.sbuf_tensor("TOT8S", [128, 8, 32], F32))
            MASKS = e(nc.sbuf_tensor("MASKS", [128, 8], F32))
        if kind != "B":
            TOTH = e(nc.sbuf_tensor("TOTH", [128, 64], F32))
            TOTS = e(nc.sbuf_tensor("TOTS", [128, 32], F32))
        if kind == "F":
            PKT = e(nc.sbuf_tensor("PKT", [128, 33], F32))
            RCV = e(nc.sbuf_tensor("RCV", [128, 66], F32))
            RCV2 = e(nc.sbuf_tensor("RCV2", [128, 66], F32))
            PRESUM = e(nc.sbuf_tensor("PRESUM", [128, 32], F32))
            MASKP = e(nc.sbuf_tensor("MASKP", [128, 8], F32))
            POWN = e(nc.sbuf_tensor("POWN", [128, 32], F32))
            MSIBC = e(nc.sbuf_tensor("MSIBC", [128, 1], F32))
            FLG = e(nc.sbuf_tensor("FLG", [1, 34], F32))
        REGS = {}
        CARREF = [0]
        EXREF = [None]

        B1b = BUF1[:].bitcast(BF16)
        B2b = BUF2[:].bitcast(BF16)
        B3b = BUF3[:].bitcast(BF16)
        B3f8 = BUF3[:].bitcast(F8)
        WHLb = WHL[:].bitcast(BF16)
        WHLf8 = WHL[:].bitcast(F8)
        TMPb = TMP[:].bitcast(BF16)

        def R(i):
            return TMP[0:1, i, :]

        pb = [e(nc.psum_tensor(f"pb{i}", [128, TH], F32)) for i in range(8)]

        ca = Ctr(stx, nc, "ca")
        cv = Ctr(stx, nc, "cv")
        ct = Ctr(stx, nc, "ct")
        cin = Ctr(stx, nc, "cin")
        cw = Ctr(stx, nc, "cw")
        cf = Ctr(stx, nc, "cf")
        cfm = Ctr(stx, nc, "cfm")
        cwm = Ctr(stx, nc, "cwm")
        cwp = [Ctr(stx, nc, "cwp0"), Ctr(stx, nc, "cwp1")]
        cwmp = [Ctr(stx, nc, "cwmp0"), Ctr(stx, nc, "cwmp1")]
        co = Ctr(stx, nc, "co")
        cmb = Ctr(stx, nc, "cmb")
        cpl = Ctr(stx, nc, "cpl")
        csp = Ctr(stx, nc, "csp")

        block = e(nc.Block())

        def sec(name, fn):
            getattr(block, name)(fn)

        def barrier(*engs):
            snap = [(c.sem, c.n) for c in (ca, cv, ct, cin, cf, cfm) if c.n > 0]
            for eng in engs:
                def f(h, snap=snap):
                    for s, n in snap:
                        h.wait_ge(s, n)
                sec(eng, f)

        def f(g):
            if kind != "F":
                for i in range(4):
                    src = xT[512 * i:512 * (i + 1), :].rearrange("(c p) t -> p c t", p=128)
                    g.dma_start(out=BUF1[:, 4 * i:4 * i + 4, :], in_=src).then_inc(cin.sem, 16)
                cin.n += 64
            g.dma_start(out=SC[:], in_=cT).then_inc(cin.sem, 16)
            cin.n += 16
            if kind == "B":
                g.dma_start(out=TOT8S[:], in_=tot8.rearrange("r p f -> p r f")).then_inc(cin.sem, 16)
                g.dma_start(out=MASKS[:], in_=mask8).then_inc(cin.sem, 16)
                cin.n += 32
            if kind == "F":
                g.dma_start(out=MASKP[:], in_=maskp).then_inc(cin.sem, 16)
                g.dma_start(out=MSIBC[:], in_=msib).then_inc(cin.sem, 16)
                cin.n += 32
        sec("gpsimd", f)
        x_loaded = cin.n

        def f(v):
            cv.inc(v.memset(ONESC[:], 1.0))
            cv.inc(v.memset(ONESR[:], 1.0))
            cv.inc(v.memset(EPSC[0:1, 0:1], EPS_LN))
            cv.inc(v.memset(EPSC[0:1, 1:2], EPS_RMS))
            cv.inc(v.memset(EPSC[0:1, 2:3], 1e-6))
            cv.inc(v.memset(EPSB[:], 1e-6))
            if kind == "F":
                cv.inc(v.memset(RCV2[:], 0.0))
            if kind == "B":
                v.wait_ge(cin.sem, x_loaded)
                cv.inc(v.memset(CAR[:], 0.0))
                for r in range(8):
                    v.wait_ge(cv.sem, cv.n)
                    cv.inc(v.scalar_tensor_tensor(
                        out=CAR[:], in0=TOT8S[:, r, :], scalar=MASKS[:, r:r + 1],
                        in1=CAR[:], op0=ALU.mult, op1=ALU.add))
        sec("vector", f)
        car_ready = cv.n

        def f(s):
            s.wait_ge(cin.sem, x_loaded)
            ca.inc(s.activation(out=SCs[:], in_=SC[:], func=AF.Silu))
            s.wait_ge(ca.sem, ca.n)
            ca.inc(s.activation(out=SCh[:], in_=SCs[:], func=AF.Copy))
        sec("scalar", f)
        sc_hi = ca.n

        def f(v):
            v.wait_ge(ca.sem, sc_hi)
            cv.inc(v.tensor_tensor(out=SCl[:], in0=SCs[:], in1=SCh[:], op=ALU.subtract))
        sec("vector", f)
        sc_ready = cv.n

        NSLAB_M = NMOD * 4
        mod_lo = [None] * NSLAB_M

        def f(g):
            for si in range(NSLAB_M):
                ngrp, ks = divmod(si, 4)
                b = si % 2
                if si >= 2:
                    g.wait_ge(cfm.sem, si - 1)
                src = wmod[512 * ks:512 * ks + 512, 512 * ngrp:512 * ngrp + 512] \
                    .rearrange("(kc p) n -> p kc n", p=128)
                dst = BUF3[:, 2 * b:2 * b + 2, :].rearrange("p c t -> p (c t)") \
                    .rearrange("p (kc n) -> p kc n", kc=4)
                g.dma_start(out=dst, in_=src).then_inc(cwmp[b].sem, 16)
                cwmp[b].n += 16
        sec("sync", f)

        def mod_conv(si):
            b = si % 2
            flat = BUF3[:, 2 * b:2 * b + 2, :].rearrange("p c t -> p (c t)")
            hi = B3b[:, 4 + 2 * b, :]
            lo = B3b[:, 5 + 2 * b, :]
            wv_ = 16 * (si // 2 + 1)

            def fs(s, si=si, flat=flat, hi=hi, wv_=wv_):
                s.wait_ge(cwmp[si % 2].sem, wv_)
                if si >= 2:
                    s.wait_ge(cfm.sem, si - 1)
                ca.inc(s.activation(out=hi, in_=flat, func=AF.Copy))
            sec("scalar", fs)
            av = ca.n

            def fv(v, si=si, flat=flat, hi=hi, lo=lo, av=av):
                v.wait_ge(ca.sem, av)
                if si >= 2:
                    v.wait_ge(cfm.sem, si - 1)
                cv.inc(v.tensor_tensor(out=lo, in0=flat, in1=hi, op=ALU.subtract))
            sec("vector", fv)
            mod_lo[si] = (av, cv.n)

        mod_conv(0)
        mod_conv(1)
        row_war = [0, 0]
        for ngrp in range(NMOD):
            for si in range(4 * ngrp + 2, min(4 * ngrp + 6, NSLAB_M)):
                if mod_lo[si] is None:
                    mod_conv(si)

            def ft(t, ngrp=ngrp):
                t.wait_ge(cv.sem, sc_ready)
                bank = pb[ngrp % 2]
                for ks in range(4):
                    si = 4 * ngrp + ks
                    b = si % 2
                    t.wait_ge(ca.sem, mod_lo[si][0])
                    t.wait_ge(cv.sem, mod_lo[si][1])
                    hi = B3b[:, 4 + 2 * b, :].rearrange("p (kc n) -> p kc n", kc=4)
                    lo = B3b[:, 5 + 2 * b, :].rearrange("p (kc n) -> p kc n", kc=4)
                    last = None
                    for kc in range(4):
                        k = 4 * ks + kc
                        st_ = (ks == 0 and kc == 0)
                        sp_ = (ks == 3 and kc == 3)
                        t.matmul(bank[0:1, :], SCh[:, k:k + 1], hi[:, kc, :], start=st_, stop=False)
                        t.matmul(bank[0:1, :], SCh[:, k:k + 1], lo[:, kc, :], start=False, stop=False)
                        last = t.matmul(bank[0:1, :], SCl[:, k:k + 1], hi[:, kc, :],
                                        start=False, stop=sp_)
                    last.then_inc(cfm.sem, 1)
                    cfm.n += 1
            sec("tensor", ft)
            grp_done = cfm.n

            scr = ngrp % 2
            def fs(s, ngrp=ngrp, scr=scr, grp_done=grp_done):
                s.wait_ge(cfm.sem, grp_done)
                if row_war[scr]:
                    s.wait_ge(ct.sem, row_war[scr])
                ca.inc(s.activation(out=R(scr), in_=pb[ngrp % 2][0:1, :], func=AF.Copy))
            sec("scalar", fs)
            av = ca.n

            def ft2(t, ngrp=ngrp, scr=scr, av=av):
                t.wait_ge(ca.sem, av)
                for j in range(4):
                    col = 4 * ngrp + j
                    last = t.matmul(pb[2][:, col:col + 1],
                                    TMP[0:1, scr, 128 * j:128 * j + 128],
                                    ONESR[0:1, 0:1], start=True, stop=True)
                ct.inc(last)
            sec("tensor", ft2)
            row_war[scr] = ct.n

        def fs(s):
            s.wait_ge(ct.sem, ct.n)
            s.wait_ge(cfm.sem, cfm.n)
            ca.inc(s.activation(out=MODC[:, 0:4 * NMOD], in_=pb[2][:, 0:4 * NMOD], func=AF.Copy))
        sec("scalar", fs)

        def fv(v):
            v.wait_ge(ca.sem, ca.n)
            cv.inc(v.tensor_scalar_add(out=MODC[:, 16:32], in0=MODC[:, 16:32], scalar1=1.0))
            if kind != "A":
                v.wait_ge(cv.sem, cv.n)
                cv.inc(v.tensor_scalar_add(out=MODC[:, 64:80], in0=MODC[:, 64:80], scalar1=1.0))
        sec("vector", fv)

        if kind != "A":
            for base in (32, 80):
                gsl = MODC[:, base:base + 16]

                def fv1(v, gsl=gsl):
                    v.wait_ge(cv.sem, cv.n)
                    v.wait_ge(ca.sem, ca.n)
                    cv.inc(v.tensor_single_scalar(out=MSCR[:, 0:16], in_=gsl,
                                                  scalar=0.0, op=ALU.is_ge))
                    v.wait_ge(cv.sem, cv.n)
                    cv.inc(v.tensor_scalar(out=MSCR[:, 0:16], in0=MSCR[:, 0:16],
                                           scalar1=2.0, scalar2=-1.0,
                                           op0=ALU.mult, op1=ALU.add))
                sec("vector", fv1)

                def fs1(s, gsl=gsl):
                    s.wait_ge(ca.sem, ca.n)
                    s.wait_ge(cv.sem, cv.n)
                    ca.inc(s.activation(out=MSCR[:, 16:32], in_=gsl, func=AF.Abs))
                sec("scalar", fs1)
                av = ca.n

                def fv2(v, gsl=gsl, av=av):
                    v.wait_ge(ca.sem, av)
                    cv.inc(v.tensor_scalar_max(out=MSCR[:, 16:32], in0=MSCR[:, 16:32],
                                               scalar1=0.25))
                    v.wait_ge(cv.sem, cv.n)
                    cv.inc(v.tensor_tensor(out=gsl, in0=MSCR[:, 0:16],
                                           in1=MSCR[:, 16:32], op=ALU.mult))
                sec("vector", fv2)

        sa_c = MODC[:, 0:16]
        ca1_c = MODC[:, 16:32]
        ga_c = MODC[:, 32:48]
        sb_c = MODC[:, 48:64]
        cb1_c = MODC[:, 64:80]
        gb_c = MODC[:, 80:96]

        def ln_stats(src_ap3, want_mean):
            tmp_free = [0, 0]
            for n in range(2):
                for fch in range(KC):
                    xs = src_ap3[:, fch, TH * n:TH * n + TH]
                    slot = fch % 2
                    tf = tmp_free[slot]

                    def fs(s, xs=xs, slot=slot, tf=tf):
                        if tf:
                            s.wait_ge(ct.sem, tf)
                        ca.inc(s.activation(out=TMP[:, slot, :], in_=xs, func=AF.Square))
                    sec("scalar", fs)
                    av = ca.n

                    def ft(t, xs=xs, slot=slot, av=av, n=n, fch=fch):
                        if want_mean:
                            t.matmul(pb[4 + n][0:1, :], ONESC[:], xs,
                                     start=(fch == 0), stop=(fch == KC - 1))
                        t.wait_ge(ca.sem, av)
                        last = t.matmul(pb[6 + n][0:1, :] if want_mean else pb[4 + n][0:1, :],
                                        ONESC[:], TMP[:, slot, :],
                                        start=(fch == 0), stop=(fch == KC - 1))
                        ct.inc(last)
                    sec("tensor", ft)
                    tmp_free[slot] = ct.n
            return ct.n

        def row_math_ln(n):
            def fs(s, n=n, can=ca.n):
                if can:
                    s.wait_ge(ca.sem, can)
                s.wait_ge(ct.sem, ct.n)
                s.wait_ge(cv.sem, cv.n)
                ca.inc(s.activation(out=R(0), in_=pb[4 + n][0:1, :],
                                    func=AF.Square, scale=1.0 / D))
            sec("scalar", fs)
            av = ca.n

            def fv(v, n=n, av=av):
                v.wait_ge(ca.sem, av)
                cv.inc(v.scalar_tensor_tensor(out=R(0), in0=pb[6 + n][0:1, :],
                                              scalar=1.0 / D, in1=R(0),
                                              op0=ALU.mult, op1=ALU.subtract))
            sec("vector", fv)
            vv = cv.n

            def fs2(s, n=n, vv=vv):
                s.wait_ge(cv.sem, vv)
                ca.inc(s.activation(out=R(1), in_=R(0), func=AF.Ln, bias=EPSC[0:1, 0:1]))
                s.wait_ge(ca.sem, ca.n)
                ca.inc(s.activation(out=R(1), in_=R(1), func=AF.Exp, scale=-0.5))
                s.wait_ge(ca.sem, ca.n)
                ca.inc(s.activation(out=R(0), in_=pb[4 + n][0:1, :],
                                    func=AF.Copy, scale=-1.0 / D))
            sec("scalar", fs2)
            av2 = ca.n

            def fv2(v, av2=av2):
                v.wait_ge(ca.sem, av2)
                cv.inc(v.tensor_tensor(out=R(0), in0=R(0), in1=R(1), op=ALU.mult))
            sec("vector", fv2)

        def bcast_rows(slots, banks):
            vv, av = cv.n, ca.n

            def ft(t):
                t.wait_ge(cv.sem, vv)
                t.wait_ge(ca.sem, av)
                last = None
                for slot, bank in zip(slots, banks):
                    last = t.matmul(pb[bank][:], ONESR[0:1, :], R(slot),
                                    start=True, stop=True)
                ct.inc(last)
            sec("tensor", ft)
            return ct.n

        slab_no = [0]
        lo_ready = {}

        def slab(dma_specs, fp8=False):
            si = slab_no[0]
            slab_no[0] += 1
            b = si % NBUF

            def fg(g, si=si, b=b):
                if si >= NBUF:
                    g.wait_ge(cf.sem, si - NBUF + 1)
                for (src, c0) in dma_specs:
                    ncol = src.shape[1] * src.shape[2]
                    dst = WST[:, b, c0:c0 + ncol].rearrange("p (kc c) -> p kc c",
                                                            kc=src.shape[1])
                    g.dma_start(out=dst, in_=src).then_inc(cwp[b].sem, 16)
                    cwp[b].n += 16
            sec("sync", fg)
            wv_ = cwp[b].n

            def fv(v, si=si, b=b, wv_=wv_, fp8=fp8):
                v.wait_ge(cwp[b].sem, wv_)
                if si >= NBUF:
                    v.wait_ge(cf.sem, si - NBUF + 1)
                if fp8:
                    cv.inc(v.tensor_copy(out=WHLf8[:, b, 0:512], in_=WST[:, b, :]))
                else:
                    cv.inc(v.tensor_copy(out=WHLb[:, b, 0:512], in_=WST[:, b, :]))
                    v.wait_ge(cv.sem, cv.n)
                    cv.inc(v.tensor_tensor(out=WHLb[:, b, 512:1024], in0=WST[:, b, :],
                                           in1=WHLb[:, b, 0:512], op=ALU.subtract))
            sec("vector", fv)
            lo_ready[si] = ("v", cv.n)
            return si

        def wait_slab(t, si):
            kc_, val = lo_ready[si]
            t.wait_ge((ca if kc_ == "a" else cv).sem, val)

        def mm3_phase(wdram, col_base, evac_fn, rhs_hi_fn, rhs_lo_fn, single_set=False):
            bank_free = {}
            for g in range(8):
                base = 0 if single_set else 4 * (g % 2)
                sis = []
                for s_ in range(8):
                    src = wdram[256 * s_:256 * s_ + 256,
                                col_base + 256 * g:col_base + 256 * g + 256] \
                        .rearrange("(kc p) c -> p kc c", p=128)
                    sis.append(slab([(src, 0)]))

                def ft(t, sis=tuple(sis), bf=dict(bank_free), base=base):
                    for bi in range(base, base + 4):
                        if bi in bf:
                            cc_, val = bf[bi]
                            t.wait_ge(cc_.sem, val)
                    last = None
                    for s_, si in enumerate(sis):
                        wait_slab(t, si)
                        b = si % NBUF
                        for kcl in range(2):
                            kcg = 2 * s_ + kcl
                            for mc in range(2):
                                o0 = 256 * kcl + 128 * mc
                                whi = WHLb[:, b, o0:o0 + 128]
                                wlo = WHLb[:, b, 512 + o0:512 + o0 + 128]
                                for n in range(2):
                                    bank = pb[base + 2 * mc + n]
                                    t.matmul(bank[:], whi, rhs_hi_fn(kcg, n),
                                             start=(kcg == 0), stop=False)
                                    t.matmul(bank[:], whi, rhs_lo_fn(kcg, n),
                                             start=False, stop=False)
                                    last = t.matmul(bank[:], wlo, rhs_hi_fn(kcg, n),
                                                    start=False, stop=(kcg == KC - 1))
                        last.then_inc(cf.sem, 1)
                        cf.n += 1
                sec("tensor", ft)
                gv = cf.n
                for mc in range(2):
                    for n in range(2):
                        bank_free[base + 2 * mc + n] = evac_fn(2 * g + mc, n,
                                                              base + 2 * mc + n, gv)

        def h_like_loop(src3, apply_fn):
            tmp_free = [(0, 0), (0, 0)]
            for n in range(2):
                row_math_ln(n)
                bc = bcast_rows((1, 0), (0, 1))
                for fch in range(KC):
                    xs = src3[:, fch, TH * n:TH * n + TH]
                    slot = fch % 2
                    fa, fvv = tmp_free[slot]

                    def fv(v, xs=xs, slot=slot, bc=bc, fa=fa, fvv=fvv):
                        v.wait_ge(ct.sem, bc)
                        if fa:
                            v.wait_ge(ca.sem, fa)
                        if fvv:
                            v.wait_ge(cv.sem, fvv)
                        cv.inc(v.tensor_tensor(out=TMP[:, slot, :], in0=xs,
                                               in1=pb[0][:], op=ALU.mult))
                        v.wait_ge(cv.sem, cv.n)
                        cv.inc(v.tensor_tensor(out=TMP[:, slot, :], in0=TMP[:, slot, :],
                                               in1=pb[1][:], op=ALU.add))
                    sec("vector", fv)
                    tmp_free[slot] = apply_fn(fch, n, slot, cv.n)


        def rhs_hi(kcg, n):
            return B1b[:, kcg, TH * n:TH * n + TH]

        def rhs_lo(kcg, n):
            return B1b[:, kcg, T + TH * n:T + TH * n + TH]


        def emit_front(accum, xld, spill=False):
            barrier("scalar", "vector", "tensor")

            def f(t):
                t.wait_ge(cin.sem, xld)
                t.wait_ge(cv.sem, car_ready)
            sec("tensor", f)
            ln_stats(BUF1[:], want_mean=True)


            h_apply_done = []

            def h_apply(fch, n, slot, vv):
                def fv(v, fch=fch, n=n, slot=slot, vv=vv):
                    v.wait_ge(cv.sem, vv)
                    cv.inc(v.tensor_scalar(out=BUF2[:, fch, TH * n:TH * n + TH],
                                           in0=TMP[:, slot, :],
                                           scalar1=ca1_c[:, fch:fch + 1],
                                           scalar2=sa_c[:, fch:fch + 1],
                                           op0=ALU.mult, op1=ALU.add))
                sec("vector", fv)
                h_apply_done.append(cv.n)
                return (0, cv.n)

            h_like_loop(BUF1[:], h_apply)

            if spill:
                for n in range(2):
                    for fch in range(KC):
                        hv = h_apply_done[16 * n + fch]

                        def fvs(v, fch=fch, n=n, hv=hv):
                            v.wait_ge(cv.sem, hv)
                            cv.inc(v.scalar_tensor_tensor(
                                out=BUF1[:, fch, TH * n:TH * n + TH],
                                in0=BUF2[:, fch, TH * n:TH * n + TH],
                                scalar=ga_c[:, fch:fch + 1],
                                in1=BUF1[:, fch, TH * n:TH * n + TH],
                                op0=ALU.mult, op1=ALU.add))
                        sec("vector", fvs)
                        vv2 = cv.n

                        def fgs(g, fch=fch, n=n, vv2=vv2):
                            g.wait_ge(cv.sem, vv2)
                            g.dma_start(out=x2pd[128 * fch:128 * fch + 128, TH * n:TH * n + TH],
                                        in_=BUF1[:, fch, TH * n:TH * n + TH]).then_inc(csp.sem, 16)
                            csp.n += 16
                        sec("gpsimd", fgs)
                spill_done = csp.n

                def fvw(v, sd=spill_done):
                    v.wait_ge(csp.sem, sd)
                sec("vector", fvw)

            barrier("scalar", "vector", "tensor")
            ln_stats(BUF2[:], want_mean=False)
            for n in range(2):
                def fs(s, n=n, can=ca.n):
                    if can:
                        s.wait_ge(ca.sem, can)
                    s.wait_ge(ct.sem, ct.n)
                    s.wait_ge(cv.sem, cv.n)
                    ca.inc(s.activation(out=R(1), in_=pb[4 + n][0:1, :],
                                        func=AF.Ln, scale=1.0 / D, bias=EPSC[0:1, 1:2]))
                    s.wait_ge(ca.sem, ca.n)
                    ca.inc(s.activation(out=R(1), in_=R(1), func=AF.Exp, scale=-0.5))
                sec("scalar", fs)
                bc = bcast_rows((1,), (0,))
                tmp_free = [(0, 0), (0, 0)]
                for fch in range(KC):
                    hsrc = BUF2[:, fch, TH * n:TH * n + TH]
                    hi = B1b[:, fch, TH * n:TH * n + TH]
                    lo = B1b[:, fch, T + TH * n:T + TH * n + TH]
                    slot = fch % 2
                    fa, fvv = tmp_free[slot]

                    def fv(v, hsrc=hsrc, slot=slot, bc=bc, fa=fa, fvv=fvv):
                        v.wait_ge(ct.sem, bc)
                        if fa:
                            v.wait_ge(ca.sem, fa)
                        if fvv:
                            v.wait_ge(cv.sem, fvv)
                        cv.inc(v.tensor_tensor(out=TMP[:, slot, :], in0=hsrc,
                                               in1=pb[0][:], op=ALU.mult))
                    sec("vector", fv)
                    vv = cv.n

                    def fv2(v, hi=hi, lo=lo, slot=slot, vv=vv):
                        v.wait_ge(cv.sem, vv)
                        cv.inc(v.tensor_copy(out=hi, in_=TMP[:, slot, :]))
                        v.wait_ge(cv.sem, cv.n)
                        cv.inc(v.tensor_tensor(out=lo, in0=TMP[:, slot, :], in1=hi,
                                               op=ALU.subtract))
                    sec("vector", fv2)
                    tmp_free[slot] = (0, cv.n)

            barrier("scalar", "vector", "tensor")

            sq_war = [0, 0]

            def k_evac(oc, n, bank_i, gv):
                def fv(v, oc=oc, n=n, bank_i=bank_i, gv=gv):
                    v.wait_ge(cf.sem, gv)
                    cv.inc(v.tensor_copy(out=BUF2[:, oc, TH * n:TH * n + TH],
                                         in_=pb[bank_i][:]))
                sec("vector", fv)
                slot = bank_i % 2
                sw_ = sq_war[slot]

                def fs(s, bank_i=bank_i, gv=gv, slot=slot, sw_=sw_):
                    s.wait_ge(cf.sem, gv)
                    if sw_:
                        s.wait_ge(ct.sem, sw_)
                    ca.inc(s.activation(out=TMP[:, slot, :], in_=pb[bank_i][:],
                                        func=AF.Square))
                sec("scalar", fs)
                av = ca.n

                def ft(t, oc=oc, n=n, slot=slot, av=av):
                    t.wait_ge(ca.sem, av)
                    last = t.matmul(pb[4 + n][0:1, :], ONESC[:], TMP[:, slot, :],
                                    start=(oc == 0), stop=(oc == KC - 1))
                    ct.inc(last)
                sec("tensor", ft)
                sq_war[slot] = ct.n
                return (cv, cv.n)

            mm3_phase(wqkvT, D, k_evac, rhs_hi, rhs_lo, single_set=True)

            barrier("scalar", "vector", "tensor")
            for n in range(2):
                def fs(s, n=n, can=ca.n):
                    if can:
                        s.wait_ge(ca.sem, can)
                    s.wait_ge(ct.sem, ct.n)
                    s.wait_ge(cv.sem, cv.n)
                    ca.inc(s.activation(out=R(1), in_=pb[4 + n][0:1, :],
                                        func=AF.Ln, scale=1.0 / D, bias=EPSC[0:1, 1:2]))
                    s.wait_ge(ca.sem, ca.n)
                    ca.inc(s.activation(out=R(1), in_=R(1), func=AF.Exp, scale=-0.5))
                sec("scalar", fs)
                bc = bcast_rows((1,), (0,))
                tmp_free = [0, 0]
                for fch in range(KC):
                    ksrc = BUF2[:, fch, TH * n:TH * n + TH]
                    slot = fch % 2
                    fa = tmp_free[slot]

                    def fv(v, ksrc=ksrc, slot=slot, bc=bc, fa=fa):
                        v.wait_ge(ct.sem, bc)
                        if fa:
                            v.wait_ge(ca.sem, fa)
                        cv.inc(v.tensor_tensor(out=TMP[:, slot, :], in0=ksrc,
                                               in1=pb[0][:], op=ALU.mult))
                    sec("vector", fv)
                    vv = cv.n

                    def fs2(s, ksrc=ksrc, slot=slot, n=n, fch=fch, vv=vv):
                        s.wait_ge(cv.sem, vv)
                        if accum:
                            ca.inc(s.activation(out=ksrc, in_=TMP[:, slot, :], func=AF.Exp,
                                                accum_out=TOTH[:, 16 * n + fch:16 * n + fch + 1]))
                        else:
                            ca.inc(s.activation(out=ksrc, in_=TMP[:, slot, :], func=AF.Exp))
                    sec("scalar", fs2)
                    tmp_free[slot] = ca.n
            w_all = ca.n

            barrier("scalar", "vector", "tensor")

            def v_evac(oc, n, bank_i, gv):
                def fv(v, oc=oc, n=n, bank_i=bank_i, gv=gv):
                    v.wait_ge(cf.sem, gv)
                    v.wait_ge(ca.sem, w_all)
                    kwargs = {}
                    if accum:
                        kwargs["accum_out"] = TOTH[:, 32 + 16 * n + oc:33 + 16 * n + oc]
                    cv.inc(v.scalar_tensor_tensor(
                        out=BUF3[:, oc, TH * n:TH * n + TH], in0=pb[bank_i][:],
                        scalar=1.0, in1=BUF2[:, oc, TH * n:TH * n + TH],
                        op0=ALU.mult, op1=ALU.mult, **kwargs))
                sec("vector", fv)
                return (cv, cv.n)

            mm3_phase(wqkvT, 2 * D, v_evac, rhs_hi, rhs_lo)


        def emit_tots():
            barrier("vector")

            def fv(v):
                cv.inc(v.tensor_tensor(out=TOTS[:, 0:16], in0=TOTH[:, 0:16],
                                       in1=TOTH[:, 16:32], op=ALU.add))
                v.wait_ge(cv.sem, cv.n)
                cv.inc(v.tensor_tensor(out=TOTS[:, 16:32], in0=TOTH[:, 32:48],
                                       in1=TOTH[:, 48:64], op=ALU.add))
            sec("vector", fv)
            return cv.n

        if kind == "A":
            emit_front(True, x_loaded)
            vv = emit_tots()

            def fg(g, vv=vv):
                g.wait_ge(cv.sem, vv)
                g.dma_start(out=tot, in_=TOTS[:]).then_inc(co.sem, 16)
                co.n += 16
                g.wait_ge(co.sem, co.n)
            sec("sync", fg)
            return nc

        if kind == "B":
            emit_front(False, x_loaded, spill=True)

        if kind == "F":
            MAGICF = 123456.0
            magic_bits = np.float32(MAGICF).view(np.int32).item()
            one_bits = np.float32(1.0).view(np.int32).item()

            def fg(g):
                g.wait_ge(cin.sem, x_loaded)
                g.wait_ge(cv.sem, car_ready)
                preg = g.alloc_register("parity")
                g.load(preg, MSIBC[0:1, 0:1].bitcast(mybir.dt.int32))
                REGS["p"] = preg
                with g.If_eq(preg, one_bits):
                    g.dma_start(out=shx[1][:, 0:66], in_=RCV2[:]).then_inc(cmb.sem, 16)
                with g.Else():
                    g.dma_start(out=shx[0][:, 0:66], in_=RCV2[:]).then_inc(cmb.sem, 16)
                cmb.n += 16
                g.wait_ge(cmb.sem, cmb.n)
            sec("gpsimd", fg)

            def fv(v):
                cv.inc(v.memset(PRESUM[:], 0.0))
                cv.inc(v.memset(POWN[:], 0.0))
            sec("vector", fv)

            import os as _os
            NPRE = int(_os.environ.get("KN_NPRE", "4"))
            for b in range(NPRE):
                def fg(g, b=b):
                    g.wait_ge(cf.sem, cf.n)
                    for i in range(4):
                        srcp = xpre[512 * i:512 * (i + 1), 1024 * b:1024 * (b + 1)] \
                            .rearrange("(c p) t -> p c t", p=128)
                        g.dma_start(out=BUF1[:, 4 * i:4 * i + 4, :], in_=srcp).then_inc(cin.sem, 16)
                    cin.n += 64
                sec("gpsimd", fg)
                emit_front(True, cin.n)
                emit_tots()

                def fv(v, b=b):
                    v.wait_ge(cv.sem, cv.n)
                    cv.inc(v.scalar_tensor_tensor(out=PRESUM[:], in0=TOTS[:],
                                                  scalar=MASKP[:, b:b + 1], in1=PRESUM[:],
                                                  op0=ALU.mult, op1=ALU.add))
                    v.wait_ge(cv.sem, cv.n)
                    cv.inc(v.scalar_tensor_tensor(out=POWN[:], in0=TOTS[:],
                                                  scalar=MASKP[:, 4 + b:5 + b], in1=POWN[:],
                                                  op0=ALU.mult, op1=ALU.add))
                sec("vector", fv)

            def fv(v):
                v.wait_ge(cv.sem, cv.n)
                cv.inc(v.tensor_copy(out=PKT[:, 0:32], in_=PRESUM[:]))
                cv.inc(v.memset(PKT[:, 32:33], MAGICF))
            sec("vector", fv)
            pkt_v = cv.n

            def fg(g):
                g.wait_ge(cv.sem, pkt_v)
                preg = REGS["p"]
                with g.If_eq(preg, one_bits):
                    g.dma_start(out=shx[1][:, 0:33], in_=PKT[:]).then_inc(cmb.sem, 16)
                with g.Else():
                    g.dma_start(out=shx[0][:, 0:33], in_=PKT[:]).then_inc(cmb.sem, 16)
                cmb.n += 16
                g.wait_ge(cmb.sem, cmb.n)
            sec("gpsimd", fg)

            def fg(g):
                g.wait_ge(cf.sem, cf.n)
                for i in range(4):
                    srcp = xT[512 * i:512 * (i + 1), :].rearrange("(c p) t -> p c t", p=128)
                    g.dma_start(out=BUF1[:, 4 * i:4 * i + 4, :], in_=srcp).then_inc(cin.sem, 16)
                cin.n += 64
            sec("gpsimd", fg)
            emit_front(False, cin.n, spill=True)

            def _fg_poll(g):
                preg = REGS["p"]
                raw1 = g.alloc_register("praw1")
                raw2 = g.alloc_register("praw2")
                ready = g.alloc_register("pready")
                g.reg_mov(ready, 0)
                NPOLL = 1536
                nd = 0
                for i in range(NPOLL):
                    reps = 1 if i < 256 else 8
                    with g.If_eq(ready, 0):
                        for _r in range(reps):
                            with g.If_eq(preg, one_bits):
                                g.dma_start(out=FLG[:, 0:1], in_=shx[0][0:1, 32:33]).then_inc(cpl.sem, 16)
                            with g.Else():
                                g.dma_start(out=FLG[:, 0:1], in_=shx[1][0:1, 32:33]).then_inc(cpl.sem, 16)
                            nd += 1
                            g.wait_ge(cpl.sem, 16 * nd)
                        g.load(raw1, FLG[0:1, 0:1].bitcast(mybir.dt.int32))
                        with g.If_eq(raw1, magic_bits):
                            g.reg_mov(ready, 1)
                with g.If_eq(ready, 0):
                    g.halt()
                with g.If_eq(preg, one_bits):
                    g.dma_start(out=RCV[:, 0:33], in_=shx[0][:, 0:33]).then_inc(cmb.sem, 16)
                with g.Else():
                    g.dma_start(out=RCV[:, 0:33], in_=shx[1][:, 0:33]).then_inc(cmb.sem, 16)
                cmb.n += 16
                g.wait_ge(cmb.sem, cmb.n)

            def emit_exchange():
                sec("gpsimd", _fg_poll)
                rcv_done = cmb.n

                def fv(v):
                    v.wait_ge(cmb.sem, rcv_done)
                    cv.inc(v.tensor_copy(out=RCV2[:, 0:33].bitcast(mybir.dt.int32),
                                         in_=RCV[:, 0:33].bitcast(mybir.dt.int32)))
                    v.wait_ge(cv.sem, cv.n)
                    cv.inc(v.tensor_tensor(out=CAR[:], in0=POWN[:],
                                           in1=RCV2[:, 0:32], op=ALU.add))
                sec("vector", fv)
                CARREF[0] = cv.n
            EXREF[0] = emit_exchange

        barrier("scalar", "vector", "tensor")

        def q1_evac(oc, n, bank_i, gv):
            ctn = ct.n
            def fs(s, bank_i=bank_i, gv=gv, ctn=ctn):
                s.wait_ge(cf.sem, gv)
                if ctn:
                    s.wait_ge(ct.sem, ctn)
                ca.inc(s.activation(out=TMP[:, bank_i % 2, :], in_=pb[bank_i][:],
                                    func=AF.Square))
            sec("scalar", fs)
            av = ca.n

            def ft(t, oc=oc, n=n, bank_i=bank_i, av=av):
                t.wait_ge(ca.sem, av)
                last = t.matmul(pb[4 + n][0:1, :], ONESC[:], TMP[:, bank_i % 2, :],
                                start=(oc == 0), stop=(oc == KC - 1))
                ct.inc(last)
            sec("tensor", ft)
            return (ct, ct.n)

        mm3_phase(wqkvT, 0, q1_evac, rhs_hi, rhs_lo, single_set=True)

        if kind == "F":
            EXREF[0]()

        barrier("vector")
        scan_done = []
        for fch in range(KC):
            def fv(v, fch=fch):
                w0 = BUF2[:, fch, 0:TH]
                w1 = BUF2[:, fch, TH:T]
                cv.inc(v.tensor_tensor_scan(out=w0, data0=w0, data1=w0,
                                            initial=CAR[:, fch:fch + 1],
                                            op0=ALU.add, op1=ALU.bypass))
                v.wait_ge(cv.sem, cv.n)
                cv.inc(v.tensor_tensor_scan(out=w1, data0=w1, data1=w1,
                                            initial=BUF2[:, fch, TH - 1:TH],
                                            op0=ALU.add, op1=ALU.bypass))
                kv0 = BUF3[:, fch, 0:TH]
                kv1 = BUF3[:, fch, TH:T]
                cv.inc(v.tensor_tensor_scan(out=kv0, data0=kv0, data1=kv0,
                                            initial=CAR[:, 16 + fch:17 + fch],
                                            op0=ALU.add, op1=ALU.bypass))
                v.wait_ge(cv.sem, cv.n)
                cv.inc(v.tensor_tensor_scan(out=kv1, data0=kv1, data1=kv1,
                                            initial=BUF3[:, fch, TH - 1:TH],
                                            op0=ALU.add, op1=ALU.bypass))
            sec("vector", fv)
            scan_done.append(cv.n)

        def fs_ln(s):
            s.wait_ge(ca.sem, ca.n)
            for fch in range(KC):
                s.wait_ge(cv.sem, scan_done[fch])
                for n in range(2):
                    wc = BUF2[:, fch, TH * n:TH * n + TH]
                    ca.inc(s.activation(out=wc, in_=wc, func=AF.Ln, bias=EPSB[:]))
        sec("scalar", fs_ln)
        ln_all = ca.n

        def fs_exp(s):
            s.wait_ge(ca.sem, ln_all)
            for fch in range(KC):
                for n in range(2):
                    wc = BUF2[:, fch, TH * n:TH * n + TH]
                    ca.inc(s.activation(out=wc, in_=wc, func=AF.Exp, scale=-1.0))
        sec("scalar", fs_exp)
        exp_all = ca.n

        def fv_yp(v):
            v.wait_ge(ca.sem, exp_all)
            for fch in range(KC):
                for n in range(2):
                    wc = BUF2[:, fch, TH * n:TH * n + TH]
                    kcm = BUF3[:, fch, TH * n:TH * n + TH]
                    cv.inc(v.tensor_tensor(out=kcm, in0=kcm, in1=wc, op=ALU.mult))
        sec("vector", fv_yp)
        ypre_all = cv.n

        barrier("scalar")
        def fs(s, can=ca.n):
            if can:
                s.wait_ge(ca.sem, can)
            s.wait_ge(ct.sem, ct.n)
            s.wait_ge(cv.sem, cv.n)
            for n in range(2):
                ca.inc(s.activation(out=R(n), in_=pb[4 + n][0:1, :],
                                    func=AF.Ln, scale=1.0 / D, bias=EPSC[0:1, 1:2]))
                s.wait_ge(ca.sem, ca.n)
                ca.inc(s.activation(out=R(n), in_=R(n), func=AF.Exp, scale=-0.5))
        sec("scalar", fs)

        bcq = bcast_rows((0, 1), (6, 7))
        tmpq_free = [(0, 0), (0, 0)]

        def q2_evac(oc, n, bank_i, gv):
            slot = bank_i % 2
            fa, fvv = tmpq_free[slot]

            def fv(v, bank_i=bank_i, gv=gv, n=n, slot=slot, fa=fa, fvv=fvv):
                v.wait_ge(cf.sem, gv)
                if fa:
                    v.wait_ge(ca.sem, fa)
                if fvv:
                    v.wait_ge(cv.sem, fvv)
                cv.inc(v.tensor_copy(out=TMP[:, slot, :], in_=pb[bank_i][:]))
                v.wait_ge(ct.sem, bcq)
                v.wait_ge(cv.sem, cv.n)
                cv.inc(v.tensor_tensor(out=TMP[:, slot, :], in0=TMP[:, slot, :],
                                       in1=pb[6 + n][:], op=ALU.mult))
            sec("vector", fv)
            vv = cv.n

            def fs(s, slot=slot, vv=vv):
                s.wait_ge(cv.sem, vv)
                ca.inc(s.activation(out=TMP[:, slot, :], in_=TMP[:, slot, :],
                                    func=AF.Sigmoid))
            sec("scalar", fs)
            av = ca.n

            def fv2(v, oc=oc, n=n, slot=slot, av=av):
                v.wait_ge(ca.sem, av)
                v.wait_ge(cv.sem, ypre_all)
                y32 = BUF3[:, oc, TH * n:TH * n + TH]
                cv.inc(v.tensor_tensor(out=y32, in0=y32, in1=TMP[:, slot, :],
                                       op=ALU.mult))
            sec("vector", fv2)
            vv2 = cv.n
            hi = B2b[:, oc, TH * n:TH * n + TH]
            lo = B2b[:, oc, T + TH * n:T + TH * n + TH]

            def fv3(v, oc=oc, n=n, hi=hi, lo=lo, vv2=vv2):
                v.wait_ge(cv.sem, vv2)
                cv.inc(v.tensor_copy(out=hi, in_=BUF3[:, oc, TH * n:TH * n + TH]))
                v.wait_ge(cv.sem, cv.n)
                cv.inc(v.tensor_tensor(out=lo, in0=BUF3[:, oc, TH * n:TH * n + TH],
                                       in1=hi, op=ALU.subtract))
            sec("vector", fv3)
            tmpq_free[slot] = (ca.n, cv.n)
            return (cv, cv.n)

        mm3_phase(wqkvT, 0, q2_evac, rhs_hi, rhs_lo, single_set=True)
        y_all = (cv.n, cf.n)

        barrier("scalar", "vector", "tensor")

        def yrhs_hi(kcg, n):
            return B2b[:, kcg, TH * n:TH * n + TH]

        def yrhs_lo(kcg, n):
            return B2b[:, kcg, T + TH * n:T + TH * n + TH]

        delta_init = [False] * (KC * 2)
        last_sevac2 = {}
        out_bank_free = {}

        for b_ff in range(8):
            for jl in range(8):
                j = 8 * b_ff + jl
                sis = []
                for s_ in range(8):
                    srcu = wswT[256 * s_:256 * s_ + 256, 128 * j:128 * j + 128] \
                        .rearrange("(kc p) c -> p kc c", p=128)
                    srcg = wswT[256 * s_:256 * s_ + 256, FFD + 128 * j:FFD + 128 * j + 128] \
                        .rearrange("(kc p) c -> p kc c", p=128)
                    sis.append(slab([(srcu, 0), (srcg, 256)]))

                sbase = 4 * (jl % 2)
                _obf = max([v for bk, (c_, v) in out_bank_free.items()
                            if sbase <= bk < sbase + 4] or [0])

                def ft(t, sis=tuple(sis), lse=last_sevac2.get(jl % 2, (0, 0)),
                       sbase=sbase, _obf=_obf):
                    if lse[0]:
                        t.wait_ge(ca.sem, lse[0])
                    if lse[1]:
                        t.wait_ge(cv.sem, lse[1])
                    if _obf:
                        t.wait_ge(cv.sem, _obf)
                    last = None
                    for s_, si in enumerate(sis):
                        wait_slab(t, si)
                        b = si % NBUF
                        for kcl in range(2):
                            kcg = 2 * s_ + kcl
                            for un in range(2):
                                o0 = 256 * un + 128 * kcl
                                whi = WHLb[:, b, o0:o0 + 128]
                                wlo = WHLb[:, b, 512 + o0:512 + o0 + 128]
                                for n in range(2):
                                    bank = pb[sbase + 2 * un + n]
                                    t.matmul(bank[:], whi, yrhs_hi(kcg, n),
                                             start=(kcg == 0), stop=False)
                                    t.matmul(bank[:], whi, yrhs_lo(kcg, n),
                                             start=False, stop=False)
                                    last = t.matmul(bank[:], wlo, yrhs_hi(kcg, n),
                                                    start=False, stop=(kcg == KC - 1))
                        last.then_inc(cf.sem, 1)
                        cf.n += 1
                sec("tensor", ft)
                gv = cf.n

                for n in range(2):
                    def fs(s, n=n, gv=gv, sbase=sbase):
                        s.wait_ge(cf.sem, gv)
                        ca.inc(s.activation(out=TMP[:, n, :], in_=pb[sbase + 2 + n][:],
                                            func=AF.Silu))
                    sec("scalar", fs)
                    av = ca.n

                    def fv(v, n=n, av=av, sbase=sbase):
                        v.wait_ge(ca.sem, av)
                        cv.inc(v.tensor_tensor(out=TMP[:, n, :], in0=pb[sbase + n][:],
                                               in1=TMP[:, n, :], op=ALU.mult))
                    sec("vector", fv)
                    shi = B3b[:, jl, TH * n:TH * n + TH]
                    slo = B3b[:, jl, T + TH * n:T + TH * n + TH]

                    def fv2(v, shi=shi, slo=slo, n=n):
                        v.wait_ge(cv.sem, cv.n)
                        cv.inc(v.tensor_copy(out=shi, in_=TMP[:, n, :]))
                        v.wait_ge(cv.sem, cv.n)
                        cv.inc(v.tensor_tensor(out=slo, in0=TMP[:, n, :], in1=shi,
                                               op=ALU.subtract))
                    sec("vector", fv2)
                last_sevac2[jl % 2] = (ca.n, cv.n)
            s_block_done = cv.n

            for og in range(8):
                sis = []
                for ks in range(4):
                    src = woutT[1024 * b_ff + 256 * ks:1024 * b_ff + 256 * ks + 256,
                                256 * og:256 * og + 256] \
                        .rearrange("(kc p) c -> p kc c", p=128)
                    sis.append(slab([(src, 0)]))

                obase = 4 * (og % 2)

                def ft(t, sis=tuple(sis), sbd=s_block_done, obf=dict(out_bank_free),
                       obase=obase):
                    t.wait_ge(cv.sem, sbd)
                    for bi in range(obase, obase + 4):
                        if bi in obf:
                            cc_, val = obf[bi]
                            t.wait_ge(cc_.sem, val)
                    last = None
                    for ks, si in enumerate(sis):
                        wait_slab(t, si)
                        b = si % NBUF
                        for kcl in range(2):
                            kl = 2 * ks + kcl
                            for mc in range(2):
                                o0 = 256 * kcl + 128 * mc
                                whi = WHLb[:, b, o0:o0 + 128]
                                wlo = WHLb[:, b, 512 + o0:512 + o0 + 128]
                                for n in range(2):
                                    bank = pb[obase + 2 * mc + n]
                                    shi = B3b[:, kl, TH * n:TH * n + TH]
                                    slo = B3b[:, kl, T + TH * n:T + TH * n + TH]
                                    t.matmul(bank[:], whi, shi, start=(kl == 0), stop=False)
                                    t.matmul(bank[:], whi, slo, start=False, stop=False)
                                    last = t.matmul(bank[:], wlo, shi, start=False,
                                                    stop=(kl == 7))
                        last.then_inc(cf.sem, 1)
                        cf.n += 1
                sec("tensor", ft)
                gv = cf.n

                for mc in range(2):
                    for n in range(2):
                        oc = 2 * og + mc
                        dst = BUF1[:, oc, TH * n:TH * n + TH]
                        idx = 2 * oc + n

                        def fv(v, dst=dst, mc=mc, n=n, gv=gv, idx=idx, obase=obase):
                            v.wait_ge(cf.sem, gv)
                            if delta_init[idx]:
                                cv.inc(v.tensor_tensor(out=dst, in0=dst,
                                                       in1=pb[obase + 2 * mc + n][:],
                                                       op=ALU.add))
                            else:
                                cv.inc(v.tensor_copy(out=dst, in_=pb[obase + 2 * mc + n][:]))
                        sec("vector", fv)
                        out_bank_free[obase + 2 * mc + n] = (cv, cv.n)
                        delta_init[idx] = True
        delta_all = cv.n

        def fg(g):
            g.wait_ge(cf.sem, cf.n)
            for i in range(4):
                srcp = x2pd[512 * i:512 * (i + 1), :].rearrange("(c p) t -> p c t", p=128)
                g.dma_start(out=BUF2[:, 4 * i:4 * i + 4, :], in_=srcp).then_inc(cin.sem, 16)
            cin.n += 64
        sec("gpsimd", fg)
        x_re = cin.n

        barrier("vector")
        for n in range(2):
            for fch in range(KC):
                xs = BUF2[:, fch, TH * n:TH * n + TH]
                dl = BUF1[:, fch, TH * n:TH * n + TH]

                def fv(v, xs=xs, dl=dl, fch=fch):
                    v.wait_ge(cin.sem, x_re)
                    cv.inc(v.scalar_tensor_tensor(out=xs, in0=dl,
                                                  scalar=ga_c[:, fch:fch + 1], in1=xs,
                                                  op0=ALU.mult, op1=ALU.add))
                sec("vector", fv)

        barrier("scalar", "vector", "tensor")
        ln_stats(BUF2[:], want_mean=True)

        def h2_apply(fch, n, slot, vv):
            sl = 8 + fch // 4
            c0 = (fch % 4) * T + TH * n
            h28 = B3f8[:, sl, c0:c0 + TH]

            def fv(v, h28=h28, fch=fch, slot=slot, vv=vv):
                v.wait_ge(cv.sem, vv)
                cv.inc(v.tensor_scalar(out=h28, in0=TMP[:, slot, :],
                                       scalar1=cb1_c[:, fch:fch + 1],
                                       scalar2=sb_c[:, fch:fch + 1],
                                       op0=ALU.mult, op1=ALU.add))
            sec("vector", fv)
            return (0, cv.n)

        h_like_loop(BUF2[:], h2_apply)
        h2_all = ca.n

        def h2rhs(kcg, n):
            sl = 8 + kcg // 4
            c0 = (kcg % 4) * T + TH * n
            return B3f8[:, sl, c0:c0 + TH]

        barrier("scalar", "vector", "tensor")
        o_init = [False] * (KC * 2)
        last_fevac2 = {}
        lin_bank_free = {}

        for b_ff in range(8):
            for jl in range(8):
                j = 8 * b_ff + jl
                sis = []
                for gate in range(2):
                    for ks in range(4):
                        src = wg[gate, 512 * ks:512 * ks + 512, 128 * j:128 * j + 128] \
                            .rearrange("(kc p) c -> p kc c", p=128)
                        sis.append(slab([(src, 0)], fp8=True))

                fbase = 4 * (jl % 2)
                _lbf = max([v for bk, (c_, v) in lin_bank_free.items()
                            if fbase <= bk < fbase + 4] or [0])

                def ft(t, sis=tuple(sis), lfe=last_fevac2.get(jl % 2, (0, 0)),
                       fbase=fbase, _lbf=_lbf):
                    if lfe[0]:
                        t.wait_ge(ca.sem, lfe[0])
                    if lfe[1]:
                        t.wait_ge(cv.sem, lfe[1])
                    if _lbf:
                        t.wait_ge(cv.sem, _lbf)
                    last = None
                    for idx, si in enumerate(sis):
                        gate, ks = divmod(idx, 4)
                        wait_slab(t, si)
                        b = si % NBUF
                        wq8 = WHLf8[:, b, 0:512].rearrange("p (kc c) -> p kc c", kc=4)
                        for kcl in range(4):
                            kcg = 4 * ks + kcl
                            for n in range(2):
                                bank = pb[fbase + 2 * gate + n]
                                last = t.matmul(bank[:], wq8[:, kcl, :], h2rhs(kcg, n),
                                                start=(kcg == 0), stop=(kcg == KC - 1))
                        last.then_inc(cf.sem, 1)
                        cf.n += 1
                sec("tensor", ft)
                gv = cf.n

                for n in range(2):
                    f0b = TMPb[:, 0, 0:TH]
                    f1b = TMPb[:, 0, TH:2 * TH]
                    actb = TMPb[:, 1, 0:TH]
                    a8 = B3f8[:, 12 + jl // 4, (jl % 4) * T + TH * n:(jl % 4) * T + TH * n + TH]
                    lfe = last_fevac2.get(2, (0, 0))

                    def fv0(v, n=n, gv=gv, f0b=f0b, f1b=f1b, lfe=lfe, fbase=fbase):
                        v.wait_ge(cf.sem, gv)
                        if lfe[0]:
                            v.wait_ge(ca.sem, lfe[0])
                        if lfe[1]:
                            v.wait_ge(cv.sem, lfe[1])
                        cv.inc(v.tensor_copy(out=f0b, in_=pb[fbase + n][:]))
                        v.wait_ge(cv.sem, cv.n)
                        cv.inc(v.tensor_copy(out=f1b, in_=pb[fbase + 2 + n][:]))
                    sec("vector", fv0)
                    vv0 = cv.n

                    def fs(s, f0b=f0b, vv0=vv0):
                        s.wait_ge(cv.sem, vv0)
                        ca.inc(s.activation(out=f0b, in_=f0b, func=AF.Gelu_apprx_tanh))
                    sec("scalar", fs)
                    av = ca.n

                    def fv(v, f0b=f0b, f1b=f1b, actb=actb, a8=a8, av=av):
                        v.wait_ge(ca.sem, av)
                        cv.inc(v.tensor_tensor(out=actb, in0=f0b, in1=f1b, op=ALU.mult))
                        v.wait_ge(cv.sem, cv.n)
                        cv.inc(v.tensor_copy(out=a8, in_=actb))
                    sec("vector", fv)
                    last_fevac2[2] = (ca.n, cv.n)
                last_fevac2[jl % 2] = (ca.n, cv.n)
            act8_done = (ca.n, cv.n)

            for og in range(8):
                sis = []
                for ks in range(4):
                    src = wlin[1024 * b_ff + 256 * ks:1024 * b_ff + 256 * ks + 256,
                               256 * og:256 * og + 256] \
                        .rearrange("(kc p) c -> p kc c", p=128)
                    sis.append(slab([(src, 0)], fp8=True))

                obase = 4 * (og % 2)
                _gf = last_fevac2.get(og % 2, (0, 0))

                def ft(t, sis=tuple(sis), ad=act8_done, lbf=dict(lin_bank_free),
                       obase=obase, _gf=_gf):
                    t.wait_ge(ca.sem, ad[0])
                    t.wait_ge(cv.sem, ad[1])
                    if _gf[0]:
                        t.wait_ge(ca.sem, _gf[0])
                    if _gf[1]:
                        t.wait_ge(cv.sem, _gf[1])
                    for bi in range(obase, obase + 4):
                        if bi in lbf:
                            cc_, val = lbf[bi]
                            t.wait_ge(cc_.sem, val)
                    last = None
                    for ks, si in enumerate(sis):
                        wait_slab(t, si)
                        b = si % NBUF
                        wq8 = WHLf8[:, b, 0:512].rearrange("p (kc c) -> p kc c", kc=2)
                        for kcl in range(2):
                            kl = 2 * ks + kcl
                            for mc in range(2):
                                for n in range(2):
                                    bank = pb[obase + 2 * mc + n]
                                    a8s = B3f8[:, 12 + kl // 4,
                                               (kl % 4) * T + TH * n:(kl % 4) * T + TH * n + TH]
                                    last = t.matmul(bank[:], wq8[:, kcl, 128 * mc:128 * mc + 128],
                                                    a8s, start=(kl == 0), stop=(kl == 7))
                        last.then_inc(cf.sem, 1)
                        cf.n += 1
                sec("tensor", ft)
                gv = cf.n

                for mc in range(2):
                    for n in range(2):
                        oc = 2 * og + mc
                        dst = BUF1[:, oc, TH * n:TH * n + TH]
                        idx = 2 * oc + n

                        def fv(v, dst=dst, mc=mc, n=n, gv=gv, idx=idx, obase=obase):
                            v.wait_ge(cf.sem, gv)
                            if o_init[idx]:
                                cv.inc(v.tensor_tensor(out=dst, in0=dst,
                                                       in1=pb[obase + 2 * mc + n][:],
                                                       op=ALU.add))
                            else:
                                cv.inc(v.tensor_copy(out=dst, in_=pb[obase + 2 * mc + n][:]))
                        sec("vector", fv)
                        lin_bank_free[obase + 2 * mc + n] = (cv, cv.n)
                        o_init[idx] = True

        barrier("scalar", "vector")
        stage_ct = cf.n
        store_done = {}
        prev_obf = 0
        u = 0
        for fch in range(KC):
            for n in range(2):
                osrc = BUF1[:, fch, TH * n:TH * n + TH]
                x2s = BUF2[:, fch, TH * n:TH * n + TH]
                obf = TMPb[:, 0, 0:TH]
                stg = BUF3[:, 14 + (u % 2), 0:TH]
                sd = store_done.get(u - 2, 0)

                def fv_c(v, osrc=osrc, obf=obf, pv=prev_obf):
                    if pv:
                        v.wait_ge(cv.sem, pv)
                    cv.inc(v.tensor_copy(out=obf, in_=osrc))
                sec("vector", fv_c)
                av = cv.n

                def fv(v, obf=obf, x2s=x2s, stg=stg, fch=fch, av=av, sd=sd, u=u):
                    v.wait_ge(cv.sem, av)
                    if u < 2:
                        v.wait_ge(cf.sem, stage_ct)
                    if sd:
                        v.wait_ge(co.sem, sd)
                    cv.inc(v.scalar_tensor_tensor(out=stg, in0=obf,
                                                  scalar=gb_c[:, fch:fch + 1], in1=x2s,
                                                  op0=ALU.mult, op1=ALU.add))
                sec("vector", fv)
                prev_obf = cv.n
                vv = cv.n

                def fg(g, fch=fch, n=n, stg=stg, vv=vv):
                    g.wait_ge(cv.sem, vv)
                    if co.n:
                        g.wait_ge(co.sem, co.n)
                    g.dma_start(out=out[128 * fch:128 * fch + 128, TH * n:TH * n + TH],
                                in_=stg).then_inc(co.sem, 16)
                    co.n += 16
                sec("sync", fg)
                store_done[u] = co.n
                u += 1

        def fg(g):
            g.wait_ge(co.sem, co.n)
        sec("sync", fg)

    return nc


_NC_CACHE = {}
FUSED = True


def kernel(L0_attn_x, L0_attn_w_qkv, L0_attn_w_swiglu, L0_attn_w_out,
           L0_ffn_w_gating, L0_ffn_w_linear, L0_c, L0_wmod):
    x = np.ascontiguousarray(np.asarray(L0_attn_x, dtype=np.float32).reshape(8192, 2048).T)
    wqkvT = np.ascontiguousarray(np.asarray(L0_attn_w_qkv, dtype=np.float32).T)
    wswT = np.ascontiguousarray(np.asarray(L0_attn_w_swiglu, dtype=np.float32).T)
    woutT = np.ascontiguousarray(np.asarray(L0_attn_w_out, dtype=np.float32).T)
    wg = np.ascontiguousarray(np.asarray(L0_ffn_w_gating, dtype=np.float32))
    wlin = np.ascontiguousarray(np.asarray(L0_ffn_w_linear, dtype=np.float32))
    cT = np.ascontiguousarray(np.asarray(L0_c, dtype=np.float32).reshape(16, 128).T)
    wmod = np.ascontiguousarray(np.asarray(L0_wmod, dtype=np.float32))

    xsh = [np.ascontiguousarray(x[:, 1024 * i:1024 * (i + 1)]) for i in range(NCORES)]

    if FUSED:
        xpres, maskps, msibs = [], [], []
        for cid in range(NCORES):
            pidx = cid // 2
            if cid % 2 == 0:
                globs = list(range(0, 2 * pidx + 1, 2))
            else:
                globs = list(range(1, 2 * pidx, 2))
            cols = [x[:, 1024 * g:1024 * (g + 1)] for g in globs]
            cols += [np.zeros((2048, 1024), np.float32)] * (4 - len(cols))
            xpres.append(np.ascontiguousarray(np.concatenate(cols, axis=1)))
            mp = np.zeros((128, 8), np.float32)
            mp[:, :len(globs)] = 1.0
            nown = len(globs) - (1 if cid % 2 == 0 and globs else 0)
            mp[:, 4:4 + nown] = 1.0
            maskps.append(np.ascontiguousarray(mp))
            msibs.append(np.full((128, 1), float(cid % 2), np.float32))
        if "F" not in _NC_CACHE:
            _NC_CACHE["F"] = build("F")
        mapsF = [{"xT": xsh[i], "xpre": xpres[i], "maskp": maskps[i],
                  "msib": msibs[i], "wqkvT": wqkvT, "wswT": wswT, "woutT": woutT,
                  "wg": wg, "wlin": wlin, "cT": cT, "wmod": wmod}
                 for i in range(NCORES)]
        resF = run_bass_kernel_spmd(_NC_CACHE["F"], mapsF, core_ids=list(range(NCORES)))
        outs = [np.asarray(resF.results[i]["out"]) for i in range(NCORES)]
        full = np.concatenate([o.T for o in outs], axis=0)
        return full.reshape(1, 8192, 2048)

    if "A" not in _NC_CACHE:
        _NC_CACHE["A"] = build("A")
    mapsA = [{"xT": xsh[i], "wqkvT": wqkvT, "cT": cT, "wmod": wmod}
             for i in range(NCORES)]
    resA = run_bass_kernel_spmd(_NC_CACHE["A"], mapsA, core_ids=list(range(NCORES)))
    tot8 = np.ascontiguousarray(
        np.stack([np.asarray(resA.results[i]["tot"]) for i in range(NCORES)], axis=0))

    masks = [np.ascontiguousarray(
        np.tile((np.arange(8) < i).astype(np.float32), (128, 1)))
        for i in range(NCORES)]

    if "B" not in _NC_CACHE:
        _NC_CACHE["B"] = build("B")
    mapsB = [{"xT": xsh[i], "wqkvT": wqkvT, "wswT": wswT, "woutT": woutT,
              "wg": wg, "wlin": wlin, "cT": cT, "wmod": wmod,
              "tot8": tot8, "mask8": masks[i]} for i in range(NCORES)]
    resB = run_bass_kernel_spmd(_NC_CACHE["B"], mapsB, core_ids=list(range(NCORES)))

    outs = [np.asarray(resB.results[i]["out"]) for i in range(NCORES)]
    full = np.concatenate([o.T for o in outs], axis=0)
    return full.reshape(1, 8192, 2048)


# version 12
# speedup vs baseline: 1.4301x; 1.4301x over previous
import numpy as np
from contextlib import ExitStack

import concourse.bass as bass
import concourse.mybir as mybir
from concourse.bass_utils import run_bass_kernel_spmd

F32 = mybir.dt.float32
BF16 = mybir.dt.bfloat16
F8 = mybir.dt.float8e4
AF = mybir.ActivationFunctionType
ALU = mybir.AluOpType

NCORES = 8
D = 2048
KC = 16
T = 1024
TH = 512
FFD = 8192
EPS_RMS = 1.1920929e-07
EPS_LN = 1e-5
NBUF = 2


class Ctr:
    def __init__(self, stack, nc, name):
        self.sem = stack.enter_context(nc.semaphore(name))
        self.n = 0

    def inc(self, instr, k=1):
        instr.then_inc(self.sem, k)
        self.n += k
        return self.n


def build(kind):
    nc = bass.Bass("TRN2", target_bir_lowering=False,
                   detect_race_conditions=(kind != "F"))

    xT = nc.dram_tensor("xT", [D, T], F32, kind="ExternalInput").ap()
    wqkvT = nc.dram_tensor("wqkvT", [D, 3 * D], F32, kind="ExternalInput").ap()
    cT = nc.dram_tensor("cT", [128, KC], F32, kind="ExternalInput").ap()
    wmod = nc.dram_tensor("wmod", [D, 6 * D], F32, kind="ExternalInput").ap()
    if kind != "A":
        x2pd = nc.dram_tensor("x2pd", [D, T], F32, kind="Internal").ap()
        wswT = nc.dram_tensor("wswT", [D, 2 * FFD], F32, kind="ExternalInput").ap()
        woutT = nc.dram_tensor("woutT", [FFD, D], F32, kind="ExternalInput").ap()
        wg = nc.dram_tensor("wg", [2, D, FFD], F32, kind="ExternalInput").ap()
        wlin = nc.dram_tensor("wlin", [FFD, D], F32, kind="ExternalInput").ap()
        out = nc.dram_tensor("out", [D, T], F32, kind="ExternalOutput").ap()
    if kind == "B":
        tot8 = nc.dram_tensor("tot8", [8, 128, 32], F32, kind="ExternalInput").ap()
        mask8 = nc.dram_tensor("mask8", [128, 8], F32, kind="ExternalInput").ap()
    if kind == "F":
        xpre = nc.dram_tensor("xpre", [D, 4 * T], F32, kind="ExternalInput").ap()
        maskp = nc.dram_tensor("maskp", [128, 8], F32, kind="ExternalInput").ap()
        msib = nc.dram_tensor("msib", [128, 1], F32, kind="ExternalInput").ap()
        shx = nc.dram_tensor("shx", [2, 128, 68], F32, kind="Internal",
                             addr_space="Shared").ap()
    if kind == "A":
        tot = nc.dram_tensor("tot", [128, 32], F32, kind="ExternalOutput").ap()

    NMOD = 8 if kind == "A" else 24

    with ExitStack() as stx:
        e = stx.enter_context
        BUF1 = e(nc.sbuf_tensor("BUF1", [128, KC, T], F32))
        BUF2 = e(nc.sbuf_tensor("BUF2", [128, KC, T], F32))
        BUF3 = e(nc.sbuf_tensor("BUF3", [128, KC, T], F32))
        WST = e(nc.sbuf_tensor("WST", [128, NBUF, 512], F32))
        WHL = e(nc.sbuf_tensor("WHL", [128, NBUF, 512], F32))
        TMP = e(nc.sbuf_tensor("TMP", [128, 2, TH], F32))
        MODC = e(nc.sbuf_tensor("MODC", [128, 96], F32))
        SC = e(nc.sbuf_tensor("SC", [128, KC], F32))
        SCs = e(nc.sbuf_tensor("SCs", [128, KC], F32))
        SCh = e(nc.sbuf_tensor("SCh", [128, KC], BF16))
        SCl = e(nc.sbuf_tensor("SCl", [128, KC], BF16))
        ONESC = e(nc.sbuf_tensor("ONESC", [128, 1], F32))
        ONESR = e(nc.sbuf_tensor("ONESR", [1, 128], F32))
        EPSC = e(nc.sbuf_tensor("EPSC", [1, 3], F32))
        EPSB = e(nc.sbuf_tensor("EPSB", [128, 1], F32))
        if kind != "A":
            MSCR = e(nc.sbuf_tensor("MSCR", [128, 32], F32))
            CAR = e(nc.sbuf_tensor("CAR", [128, 32], F32))
        if kind == "B":
            TOT8S = e(nc.sbuf_tensor("TOT8S", [128, 8, 32], F32))
            MASKS = e(nc.sbuf_tensor("MASKS", [128, 8], F32))
        if kind != "B":
            TOTH = e(nc.sbuf_tensor("TOTH", [128, 64], F32))
            TOTS = e(nc.sbuf_tensor("TOTS", [128, 32], F32))
        if kind == "F":
            PKT = e(nc.sbuf_tensor("PKT", [128, 33], F32))
            RCV = e(nc.sbuf_tensor("RCV", [128, 33], F32))
            RCV2 = e(nc.sbuf_tensor("RCV2", [128, 33], F32))
            PRESUM = e(nc.sbuf_tensor("PRESUM", [128, 32], F32))
            MASKP = e(nc.sbuf_tensor("MASKP", [128, 8], F32))
            POWN = e(nc.sbuf_tensor("POWN", [128, 32], F32))
            MSIBC = e(nc.sbuf_tensor("MSIBC", [128, 1], F32))
            FLG = e(nc.sbuf_tensor("FLG", [1, 1], F32))
        REGS = {}
        CARREF = [0]
        EXREF = [None]

        B1b = BUF1[:].bitcast(BF16)
        B2b = BUF2[:].bitcast(BF16)
        B3b = BUF3[:].bitcast(BF16)
        B3f8 = BUF3[:].bitcast(F8)
        WHLb = WHL[:].bitcast(BF16)
        WHLf8 = WHL[:].bitcast(F8)
        TMPb = TMP[:].bitcast(BF16)

        def R(i):
            return TMP[0:1, i, :]

        pb = [e(nc.psum_tensor(f"pb{i}", [128, TH], F32)) for i in range(8)]

        ca = Ctr(stx, nc, "ca")
        cv = Ctr(stx, nc, "cv")
        ct = Ctr(stx, nc, "ct")
        cin = Ctr(stx, nc, "cin")
        cw = Ctr(stx, nc, "cw")
        cf = Ctr(stx, nc, "cf")
        cfm = Ctr(stx, nc, "cfm")
        cwm = Ctr(stx, nc, "cwm")
        cwp = [Ctr(stx, nc, "cwp0"), Ctr(stx, nc, "cwp1")]
        cwmp = [Ctr(stx, nc, "cwmp0"), Ctr(stx, nc, "cwmp1")]
        co = Ctr(stx, nc, "co")
        cmb = Ctr(stx, nc, "cmb")
        cpl = Ctr(stx, nc, "cpl")
        csp = Ctr(stx, nc, "csp")

        block = e(nc.Block())

        def sec(name, fn):
            getattr(block, name)(fn)

        def barrier(*engs):
            snap = [(c.sem, c.n) for c in (ca, cv, ct, cin, cf, cfm) if c.n > 0]
            for eng in engs:
                def f(h, snap=snap):
                    for s, n in snap:
                        h.wait_ge(s, n)
                sec(eng, f)

        def f(g):
            if kind != "F":
                for i in range(4):
                    src = xT[512 * i:512 * (i + 1), :].rearrange("(c p) t -> p c t", p=128)
                    g.dma_start(out=BUF1[:, 4 * i:4 * i + 4, :], in_=src).then_inc(cin.sem, 16)
                cin.n += 64
            g.dma_start(out=SC[:], in_=cT).then_inc(cin.sem, 16)
            cin.n += 16
            if kind == "B":
                g.dma_start(out=TOT8S[:], in_=tot8.rearrange("r p f -> p r f")).then_inc(cin.sem, 16)
                g.dma_start(out=MASKS[:], in_=mask8).then_inc(cin.sem, 16)
                cin.n += 32
            if kind == "F":
                g.dma_start(out=MASKP[:], in_=maskp).then_inc(cin.sem, 16)
                g.dma_start(out=MSIBC[:], in_=msib).then_inc(cin.sem, 16)
                cin.n += 32
        sec("gpsimd", f)
        x_loaded = cin.n

        def f(v):
            cv.inc(v.memset(ONESC[:], 1.0))
            cv.inc(v.memset(ONESR[:], 1.0))
            cv.inc(v.memset(EPSC[0:1, 0:1], EPS_LN))
            cv.inc(v.memset(EPSC[0:1, 1:2], EPS_RMS))
            cv.inc(v.memset(EPSC[0:1, 2:3], 1e-6))
            cv.inc(v.memset(EPSB[:], 1e-6))
            if kind == "F":
                cv.inc(v.memset(RCV2[:], 0.0))
            if kind == "B":
                v.wait_ge(cin.sem, x_loaded)
                cv.inc(v.memset(CAR[:], 0.0))
                for r in range(8):
                    v.wait_ge(cv.sem, cv.n)
                    cv.inc(v.scalar_tensor_tensor(
                        out=CAR[:], in0=TOT8S[:, r, :], scalar=MASKS[:, r:r + 1],
                        in1=CAR[:], op0=ALU.mult, op1=ALU.add))
        sec("vector", f)
        car_ready = cv.n

        def f(s):
            s.wait_ge(cin.sem, x_loaded)
            ca.inc(s.activation(out=SCs[:], in_=SC[:], func=AF.Silu))
            s.wait_ge(ca.sem, ca.n)
            ca.inc(s.activation(out=SCh[:], in_=SCs[:], func=AF.Copy))
        sec("scalar", f)
        sc_hi = ca.n

        def f(v):
            v.wait_ge(ca.sem, sc_hi)
            cv.inc(v.tensor_tensor(out=SCl[:], in0=SCs[:], in1=SCh[:], op=ALU.subtract))
        sec("vector", f)
        sc_ready = cv.n

        NSLAB_M = NMOD * 4
        mod_lo = [None] * NSLAB_M

        def f(g):
            for si in range(NSLAB_M):
                ngrp, ks = divmod(si, 4)
                b = si % 2
                if si >= 2:
                    g.wait_ge(cfm.sem, si - 1)
                src = wmod[512 * ks:512 * ks + 512, 512 * ngrp:512 * ngrp + 512] \
                    .rearrange("(kc p) n -> p kc n", p=128)
                dst = BUF3[:, 2 * b:2 * b + 2, :].rearrange("p c t -> p (c t)") \
                    .rearrange("p (kc n) -> p kc n", kc=4)
                g.dma_start(out=dst, in_=src).then_inc(cwmp[b].sem, 16)
                cwmp[b].n += 16
        sec("sync", f)

        def mod_conv(si):
            b = si % 2
            flat = BUF3[:, 2 * b:2 * b + 2, :].rearrange("p c t -> p (c t)")
            hi = B3b[:, 4 + 2 * b, :]
            lo = B3b[:, 5 + 2 * b, :]
            wv_ = 16 * (si // 2 + 1)

            def fs(s, si=si, flat=flat, hi=hi, wv_=wv_):
                s.wait_ge(cwmp[si % 2].sem, wv_)
                if si >= 2:
                    s.wait_ge(cfm.sem, si - 1)
                ca.inc(s.activation(out=hi, in_=flat, func=AF.Copy))
            sec("scalar", fs)
            av = ca.n

            def fv(v, si=si, flat=flat, hi=hi, lo=lo, av=av):
                v.wait_ge(ca.sem, av)
                if si >= 2:
                    v.wait_ge(cfm.sem, si - 1)
                cv.inc(v.tensor_tensor(out=lo, in0=flat, in1=hi, op=ALU.subtract))
            sec("vector", fv)
            mod_lo[si] = (av, cv.n)

        mod_conv(0)
        mod_conv(1)
        row_war = [0, 0]
        for ngrp in range(NMOD):
            for si in range(4 * ngrp + 2, min(4 * ngrp + 6, NSLAB_M)):
                if mod_lo[si] is None:
                    mod_conv(si)

            def ft(t, ngrp=ngrp):
                t.wait_ge(cv.sem, sc_ready)
                bank = pb[ngrp % 2]
                for ks in range(4):
                    si = 4 * ngrp + ks
                    b = si % 2
                    t.wait_ge(ca.sem, mod_lo[si][0])
                    t.wait_ge(cv.sem, mod_lo[si][1])
                    hi = B3b[:, 4 + 2 * b, :].rearrange("p (kc n) -> p kc n", kc=4)
                    lo = B3b[:, 5 + 2 * b, :].rearrange("p (kc n) -> p kc n", kc=4)
                    last = None
                    for kc in range(4):
                        k = 4 * ks + kc
                        st_ = (ks == 0 and kc == 0)
                        sp_ = (ks == 3 and kc == 3)
                        t.matmul(bank[0:1, :], SCh[:, k:k + 1], hi[:, kc, :], start=st_, stop=False)
                        t.matmul(bank[0:1, :], SCh[:, k:k + 1], lo[:, kc, :], start=False, stop=False)
                        last = t.matmul(bank[0:1, :], SCl[:, k:k + 1], hi[:, kc, :],
                                        start=False, stop=sp_)
                    last.then_inc(cfm.sem, 1)
                    cfm.n += 1
            sec("tensor", ft)
            grp_done = cfm.n

            scr = ngrp % 2
            def fs(s, ngrp=ngrp, scr=scr, grp_done=grp_done):
                s.wait_ge(cfm.sem, grp_done)
                if row_war[scr]:
                    s.wait_ge(ct.sem, row_war[scr])
                ca.inc(s.activation(out=R(scr), in_=pb[ngrp % 2][0:1, :], func=AF.Copy))
            sec("scalar", fs)
            av = ca.n

            def ft2(t, ngrp=ngrp, scr=scr, av=av):
                t.wait_ge(ca.sem, av)
                for j in range(4):
                    col = 4 * ngrp + j
                    last = t.matmul(pb[2][:, col:col + 1],
                                    TMP[0:1, scr, 128 * j:128 * j + 128],
                                    ONESR[0:1, 0:1], start=True, stop=True)
                ct.inc(last)
            sec("tensor", ft2)
            row_war[scr] = ct.n

        def fs(s):
            s.wait_ge(ct.sem, ct.n)
            s.wait_ge(cfm.sem, cfm.n)
            ca.inc(s.activation(out=MODC[:, 0:4 * NMOD], in_=pb[2][:, 0:4 * NMOD], func=AF.Copy))
        sec("scalar", fs)

        def fv(v):
            v.wait_ge(ca.sem, ca.n)
            cv.inc(v.tensor_scalar_add(out=MODC[:, 16:32], in0=MODC[:, 16:32], scalar1=1.0))
            if kind != "A":
                v.wait_ge(cv.sem, cv.n)
                cv.inc(v.tensor_scalar_add(out=MODC[:, 64:80], in0=MODC[:, 64:80], scalar1=1.0))
        sec("vector", fv)

        if kind != "A":
            for base in (32, 80):
                gsl = MODC[:, base:base + 16]

                def fv1(v, gsl=gsl):
                    v.wait_ge(cv.sem, cv.n)
                    v.wait_ge(ca.sem, ca.n)
                    cv.inc(v.tensor_single_scalar(out=MSCR[:, 0:16], in_=gsl,
                                                  scalar=0.0, op=ALU.is_ge))
                    v.wait_ge(cv.sem, cv.n)
                    cv.inc(v.tensor_scalar(out=MSCR[:, 0:16], in0=MSCR[:, 0:16],
                                           scalar1=2.0, scalar2=-1.0,
                                           op0=ALU.mult, op1=ALU.add))
                sec("vector", fv1)

                def fs1(s, gsl=gsl):
                    s.wait_ge(ca.sem, ca.n)
                    s.wait_ge(cv.sem, cv.n)
                    ca.inc(s.activation(out=MSCR[:, 16:32], in_=gsl, func=AF.Abs))
                sec("scalar", fs1)
                av = ca.n

                def fv2(v, gsl=gsl, av=av):
                    v.wait_ge(ca.sem, av)
                    cv.inc(v.tensor_scalar_max(out=MSCR[:, 16:32], in0=MSCR[:, 16:32],
                                               scalar1=0.25))
                    v.wait_ge(cv.sem, cv.n)
                    cv.inc(v.tensor_tensor(out=gsl, in0=MSCR[:, 0:16],
                                           in1=MSCR[:, 16:32], op=ALU.mult))
                sec("vector", fv2)

        sa_c = MODC[:, 0:16]
        ca1_c = MODC[:, 16:32]
        ga_c = MODC[:, 32:48]
        sb_c = MODC[:, 48:64]
        cb1_c = MODC[:, 64:80]
        gb_c = MODC[:, 80:96]

        def ln_stats(src_ap3, want_mean):
            tmp_free = [0, 0]
            for n in range(2):
                for fch in range(KC):
                    xs = src_ap3[:, fch, TH * n:TH * n + TH]
                    slot = fch % 2
                    tf = tmp_free[slot]

                    def fs(s, xs=xs, slot=slot, tf=tf):
                        if tf:
                            s.wait_ge(ct.sem, tf)
                        ca.inc(s.activation(out=TMP[:, slot, :], in_=xs, func=AF.Square))
                    sec("scalar", fs)
                    av = ca.n

                    def ft(t, xs=xs, slot=slot, av=av, n=n, fch=fch):
                        if want_mean:
                            t.matmul(pb[4 + n][0:1, :], ONESC[:], xs,
                                     start=(fch == 0), stop=(fch == KC - 1))
                        t.wait_ge(ca.sem, av)
                        last = t.matmul(pb[6 + n][0:1, :] if want_mean else pb[4 + n][0:1, :],
                                        ONESC[:], TMP[:, slot, :],
                                        start=(fch == 0), stop=(fch == KC - 1))
                        ct.inc(last)
                    sec("tensor", ft)
                    tmp_free[slot] = ct.n
            return ct.n

        def row_math_ln(n):
            def fs(s, n=n, can=ca.n):
                if can:
                    s.wait_ge(ca.sem, can)
                s.wait_ge(ct.sem, ct.n)
                s.wait_ge(cv.sem, cv.n)
                ca.inc(s.activation(out=R(0), in_=pb[4 + n][0:1, :],
                                    func=AF.Square, scale=1.0 / D))
            sec("scalar", fs)
            av = ca.n

            def fv(v, n=n, av=av):
                v.wait_ge(ca.sem, av)
                cv.inc(v.scalar_tensor_tensor(out=R(0), in0=pb[6 + n][0:1, :],
                                              scalar=1.0 / D, in1=R(0),
                                              op0=ALU.mult, op1=ALU.subtract))
            sec("vector", fv)
            vv = cv.n

            def fs2(s, n=n, vv=vv):
                s.wait_ge(cv.sem, vv)
                ca.inc(s.activation(out=R(1), in_=R(0), func=AF.Ln, bias=EPSC[0:1, 0:1]))
                s.wait_ge(ca.sem, ca.n)
                ca.inc(s.activation(out=R(1), in_=R(1), func=AF.Exp, scale=-0.5))
                s.wait_ge(ca.sem, ca.n)
                ca.inc(s.activation(out=R(0), in_=pb[4 + n][0:1, :],
                                    func=AF.Copy, scale=-1.0 / D))
            sec("scalar", fs2)
            av2 = ca.n

            def fv2(v, av2=av2):
                v.wait_ge(ca.sem, av2)
                cv.inc(v.tensor_tensor(out=R(0), in0=R(0), in1=R(1), op=ALU.mult))
            sec("vector", fv2)

        def bcast_rows(slots, banks):
            vv, av = cv.n, ca.n

            def ft(t):
                t.wait_ge(cv.sem, vv)
                t.wait_ge(ca.sem, av)
                last = None
                for slot, bank in zip(slots, banks):
                    last = t.matmul(pb[bank][:], ONESR[0:1, :], R(slot),
                                    start=True, stop=True)
                ct.inc(last)
            sec("tensor", ft)
            return ct.n

        slab_no = [0]
        lo_ready = {}

        def slab(dma_specs, fp8=False):
            si = slab_no[0]
            slab_no[0] += 1
            b = si % NBUF

            def fg(g, si=si, b=b):
                if si >= NBUF:
                    g.wait_ge(cf.sem, si - NBUF + 1)
                for (src, c0) in dma_specs:
                    ncol = src.shape[1] * src.shape[2]
                    dst = WST[:, b, c0:c0 + ncol].rearrange("p (kc c) -> p kc c",
                                                            kc=src.shape[1])
                    g.dma_start(out=dst, in_=src).then_inc(cwp[b].sem, 16)
                    cwp[b].n += 16
            sec("sync", fg)
            wv_ = cwp[b].n

            def fv(v, si=si, b=b, wv_=wv_, fp8=fp8):
                v.wait_ge(cwp[b].sem, wv_)
                if si >= NBUF:
                    v.wait_ge(cf.sem, si - NBUF + 1)
                if fp8:
                    cv.inc(v.tensor_copy(out=WHLf8[:, b, 0:512], in_=WST[:, b, :]))
                else:
                    cv.inc(v.tensor_copy(out=WHLb[:, b, 0:512], in_=WST[:, b, :]))
                    v.wait_ge(cv.sem, cv.n)
                    cv.inc(v.tensor_tensor(out=WHLb[:, b, 512:1024], in0=WST[:, b, :],
                                           in1=WHLb[:, b, 0:512], op=ALU.subtract))
            sec("vector", fv)
            lo_ready[si] = ("v", cv.n)
            return si

        def wait_slab(t, si):
            kc_, val = lo_ready[si]
            t.wait_ge((ca if kc_ == "a" else cv).sem, val)

        def mm3_phase(wdram, col_base, evac_fn, rhs_hi_fn, rhs_lo_fn, single_set=False):
            bank_free = {}
            for g in range(8):
                base = 0 if single_set else 4 * (g % 2)
                sis = []
                for s_ in range(8):
                    src = wdram[256 * s_:256 * s_ + 256,
                                col_base + 256 * g:col_base + 256 * g + 256] \
                        .rearrange("(kc p) c -> p kc c", p=128)
                    sis.append(slab([(src, 0)]))

                def ft(t, sis=tuple(sis), bf=dict(bank_free), base=base):
                    for bi in range(base, base + 4):
                        if bi in bf:
                            cc_, val = bf[bi]
                            t.wait_ge(cc_.sem, val)
                    last = None
                    for s_, si in enumerate(sis):
                        wait_slab(t, si)
                        b = si % NBUF
                        for kcl in range(2):
                            kcg = 2 * s_ + kcl
                            for mc in range(2):
                                o0 = 256 * kcl + 128 * mc
                                whi = WHLb[:, b, o0:o0 + 128]
                                wlo = WHLb[:, b, 512 + o0:512 + o0 + 128]
                                for n in range(2):
                                    bank = pb[base + 2 * mc + n]
                                    t.matmul(bank[:], whi, rhs_hi_fn(kcg, n),
                                             start=(kcg == 0), stop=False)
                                    t.matmul(bank[:], whi, rhs_lo_fn(kcg, n),
                                             start=False, stop=False)
                                    last = t.matmul(bank[:], wlo, rhs_hi_fn(kcg, n),
                                                    start=False, stop=(kcg == KC - 1))
                        last.then_inc(cf.sem, 1)
                        cf.n += 1
                sec("tensor", ft)
                gv = cf.n
                for mc in range(2):
                    for n in range(2):
                        bank_free[base + 2 * mc + n] = evac_fn(2 * g + mc, n,
                                                              base + 2 * mc + n, gv)

        def h_like_loop(src3, apply_fn):
            tmp_free = [(0, 0), (0, 0)]
            for n in range(2):
                row_math_ln(n)
                bc = bcast_rows((1, 0), (0, 1))
                for fch in range(KC):
                    xs = src3[:, fch, TH * n:TH * n + TH]
                    slot = fch % 2
                    fa, fvv = tmp_free[slot]

                    def fv(v, xs=xs, slot=slot, bc=bc, fa=fa, fvv=fvv):
                        v.wait_ge(ct.sem, bc)
                        if fa:
                            v.wait_ge(ca.sem, fa)
                        if fvv:
                            v.wait_ge(cv.sem, fvv)
                        cv.inc(v.tensor_tensor(out=TMP[:, slot, :], in0=xs,
                                               in1=pb[0][:], op=ALU.mult))
                        v.wait_ge(cv.sem, cv.n)
                        cv.inc(v.tensor_tensor(out=TMP[:, slot, :], in0=TMP[:, slot, :],
                                               in1=pb[1][:], op=ALU.add))
                    sec("vector", fv)
                    tmp_free[slot] = apply_fn(fch, n, slot, cv.n)


        def rhs_hi(kcg, n):
            return B1b[:, kcg, TH * n:TH * n + TH]

        def rhs_lo(kcg, n):
            return B1b[:, kcg, T + TH * n:T + TH * n + TH]


        def emit_front(accum, xld, spill=False):
            barrier("scalar", "vector", "tensor")

            def f(t):
                t.wait_ge(cin.sem, xld)
                t.wait_ge(cv.sem, car_ready)
            sec("tensor", f)
            ln_stats(BUF1[:], want_mean=True)


            h_apply_done = []

            def h_apply(fch, n, slot, vv):
                def fv(v, fch=fch, n=n, slot=slot, vv=vv):
                    v.wait_ge(cv.sem, vv)
                    cv.inc(v.tensor_scalar(out=BUF2[:, fch, TH * n:TH * n + TH],
                                           in0=TMP[:, slot, :],
                                           scalar1=ca1_c[:, fch:fch + 1],
                                           scalar2=sa_c[:, fch:fch + 1],
                                           op0=ALU.mult, op1=ALU.add))
                sec("vector", fv)
                h_apply_done.append(cv.n)
                return (0, cv.n)

            h_like_loop(BUF1[:], h_apply)

            if spill:
                for n in range(2):
                    for fch in range(KC):
                        hv = h_apply_done[16 * n + fch]

                        def fvs(v, fch=fch, n=n, hv=hv):
                            v.wait_ge(cv.sem, hv)
                            cv.inc(v.scalar_tensor_tensor(
                                out=BUF1[:, fch, TH * n:TH * n + TH],
                                in0=BUF2[:, fch, TH * n:TH * n + TH],
                                scalar=ga_c[:, fch:fch + 1],
                                in1=BUF1[:, fch, TH * n:TH * n + TH],
                                op0=ALU.mult, op1=ALU.add))
                        sec("vector", fvs)
                        vv2 = cv.n

                        def fgs(g, fch=fch, n=n, vv2=vv2):
                            g.wait_ge(cv.sem, vv2)
                            g.dma_start(out=x2pd[128 * fch:128 * fch + 128, TH * n:TH * n + TH],
                                        in_=BUF1[:, fch, TH * n:TH * n + TH]).then_inc(csp.sem, 16)
                            csp.n += 16
                        sec("gpsimd", fgs)
                spill_done = csp.n

                def fvw(v, sd=spill_done):
                    v.wait_ge(csp.sem, sd)
                sec("vector", fvw)

            barrier("scalar", "vector", "tensor")
            ln_stats(BUF2[:], want_mean=False)
            for n in range(2):
                def fs(s, n=n, can=ca.n):
                    if can:
                        s.wait_ge(ca.sem, can)
                    s.wait_ge(ct.sem, ct.n)
                    s.wait_ge(cv.sem, cv.n)
                    ca.inc(s.activation(out=R(1), in_=pb[4 + n][0:1, :],
                                        func=AF.Ln, scale=1.0 / D, bias=EPSC[0:1, 1:2]))
                    s.wait_ge(ca.sem, ca.n)
                    ca.inc(s.activation(out=R(1), in_=R(1), func=AF.Exp, scale=-0.5))
                sec("scalar", fs)
                bc = bcast_rows((1,), (0,))
                tmp_free = [(0, 0), (0, 0)]
                for fch in range(KC):
                    hsrc = BUF2[:, fch, TH * n:TH * n + TH]
                    hi = B1b[:, fch, TH * n:TH * n + TH]
                    lo = B1b[:, fch, T + TH * n:T + TH * n + TH]
                    slot = fch % 2
                    fa, fvv = tmp_free[slot]

                    def fv(v, hsrc=hsrc, slot=slot, bc=bc, fa=fa, fvv=fvv):
                        v.wait_ge(ct.sem, bc)
                        if fa:
                            v.wait_ge(ca.sem, fa)
                        if fvv:
                            v.wait_ge(cv.sem, fvv)
                        cv.inc(v.tensor_tensor(out=TMP[:, slot, :], in0=hsrc,
                                               in1=pb[0][:], op=ALU.mult))
                    sec("vector", fv)
                    vv = cv.n

                    def fv2(v, hi=hi, lo=lo, slot=slot, vv=vv):
                        v.wait_ge(cv.sem, vv)
                        cv.inc(v.tensor_copy(out=hi, in_=TMP[:, slot, :]))
                        v.wait_ge(cv.sem, cv.n)
                        cv.inc(v.tensor_tensor(out=lo, in0=TMP[:, slot, :], in1=hi,
                                               op=ALU.subtract))
                    sec("vector", fv2)
                    tmp_free[slot] = (0, cv.n)

            barrier("scalar", "vector", "tensor")

            sq_war = [0, 0]

            def k_evac(oc, n, bank_i, gv):
                def fv(v, oc=oc, n=n, bank_i=bank_i, gv=gv):
                    v.wait_ge(cf.sem, gv)
                    cv.inc(v.tensor_copy(out=BUF2[:, oc, TH * n:TH * n + TH],
                                         in_=pb[bank_i][:]))
                sec("vector", fv)
                slot = bank_i % 2
                sw_ = sq_war[slot]

                def fs(s, bank_i=bank_i, gv=gv, slot=slot, sw_=sw_):
                    s.wait_ge(cf.sem, gv)
                    if sw_:
                        s.wait_ge(ct.sem, sw_)
                    ca.inc(s.activation(out=TMP[:, slot, :], in_=pb[bank_i][:],
                                        func=AF.Square))
                sec("scalar", fs)
                av = ca.n

                def ft(t, oc=oc, n=n, slot=slot, av=av):
                    t.wait_ge(ca.sem, av)
                    last = t.matmul(pb[4 + n][0:1, :], ONESC[:], TMP[:, slot, :],
                                    start=(oc == 0), stop=(oc == KC - 1))
                    ct.inc(last)
                sec("tensor", ft)
                sq_war[slot] = ct.n
                return (cv, cv.n)

            mm3_phase(wqkvT, D, k_evac, rhs_hi, rhs_lo, single_set=True)

            barrier("scalar", "vector", "tensor")
            for n in range(2):
                def fs(s, n=n, can=ca.n):
                    if can:
                        s.wait_ge(ca.sem, can)
                    s.wait_ge(ct.sem, ct.n)
                    s.wait_ge(cv.sem, cv.n)
                    ca.inc(s.activation(out=R(1), in_=pb[4 + n][0:1, :],
                                        func=AF.Ln, scale=1.0 / D, bias=EPSC[0:1, 1:2]))
                    s.wait_ge(ca.sem, ca.n)
                    ca.inc(s.activation(out=R(1), in_=R(1), func=AF.Exp, scale=-0.5))
                sec("scalar", fs)
                bc = bcast_rows((1,), (0,))
                tmp_free = [0, 0]
                for fch in range(KC):
                    ksrc = BUF2[:, fch, TH * n:TH * n + TH]
                    slot = fch % 2
                    fa = tmp_free[slot]

                    def fv(v, ksrc=ksrc, slot=slot, bc=bc, fa=fa):
                        v.wait_ge(ct.sem, bc)
                        if fa:
                            v.wait_ge(ca.sem, fa)
                        cv.inc(v.tensor_tensor(out=TMP[:, slot, :], in0=ksrc,
                                               in1=pb[0][:], op=ALU.mult))
                    sec("vector", fv)
                    vv = cv.n

                    def fs2(s, ksrc=ksrc, slot=slot, n=n, fch=fch, vv=vv):
                        s.wait_ge(cv.sem, vv)
                        if accum:
                            ca.inc(s.activation(out=ksrc, in_=TMP[:, slot, :], func=AF.Exp,
                                                accum_out=TOTH[:, 16 * n + fch:16 * n + fch + 1]))
                        else:
                            ca.inc(s.activation(out=ksrc, in_=TMP[:, slot, :], func=AF.Exp))
                    sec("scalar", fs2)
                    tmp_free[slot] = ca.n
            w_all = ca.n

            barrier("scalar", "vector", "tensor")

            def v_evac(oc, n, bank_i, gv):
                def fv(v, oc=oc, n=n, bank_i=bank_i, gv=gv):
                    v.wait_ge(cf.sem, gv)
                    v.wait_ge(ca.sem, w_all)
                    kwargs = {}
                    if accum:
                        kwargs["accum_out"] = TOTH[:, 32 + 16 * n + oc:33 + 16 * n + oc]
                    cv.inc(v.scalar_tensor_tensor(
                        out=BUF3[:, oc, TH * n:TH * n + TH], in0=pb[bank_i][:],
                        scalar=1.0, in1=BUF2[:, oc, TH * n:TH * n + TH],
                        op0=ALU.mult, op1=ALU.mult, **kwargs))
                sec("vector", fv)
                return (cv, cv.n)

            mm3_phase(wqkvT, 2 * D, v_evac, rhs_hi, rhs_lo)


        def emit_tots():
            barrier("vector")

            def fv(v):
                cv.inc(v.tensor_tensor(out=TOTS[:, 0:16], in0=TOTH[:, 0:16],
                                       in1=TOTH[:, 16:32], op=ALU.add))
                v.wait_ge(cv.sem, cv.n)
                cv.inc(v.tensor_tensor(out=TOTS[:, 16:32], in0=TOTH[:, 32:48],
                                       in1=TOTH[:, 48:64], op=ALU.add))
            sec("vector", fv)
            return cv.n

        if kind == "A":
            emit_front(True, x_loaded)
            vv = emit_tots()

            def fg(g, vv=vv):
                g.wait_ge(cv.sem, vv)
                g.dma_start(out=tot, in_=TOTS[:]).then_inc(co.sem, 16)
                co.n += 16
                g.wait_ge(co.sem, co.n)
            sec("sync", fg)
            return nc

        if kind == "B":
            emit_front(False, x_loaded, spill=True)

        if kind == "F":
            MAGICF = 123456.0
            magic_bits = np.float32(MAGICF).view(np.int32).item()
            one_bits = np.float32(1.0).view(np.int32).item()

            def fg(g):
                g.wait_ge(cin.sem, x_loaded)
                g.wait_ge(cv.sem, car_ready)
                preg = g.alloc_register("parity")
                g.load(preg, MSIBC[0:1, 0:1].bitcast(mybir.dt.int32))
                REGS["p"] = preg
                with g.If_eq(preg, one_bits):
                    g.dma_start(out=shx[1][:, 0:33], in_=RCV2[:]).then_inc(cmb.sem, 16)
                with g.Else():
                    g.dma_start(out=shx[0][:, 0:33], in_=RCV2[:]).then_inc(cmb.sem, 16)
                cmb.n += 16
                g.wait_ge(cmb.sem, cmb.n)
            sec("gpsimd", fg)

            def fv(v):
                cv.inc(v.memset(PRESUM[:], 0.0))
                cv.inc(v.memset(POWN[:], 0.0))
            sec("vector", fv)

            import os as _os
            NPRE = int(_os.environ.get("KN_NPRE", "4"))
            for b in range(NPRE):
                def fg(g, b=b):
                    g.wait_ge(cf.sem, cf.n)
                    for i in range(4):
                        srcp = xpre[512 * i:512 * (i + 1), 1024 * b:1024 * (b + 1)] \
                            .rearrange("(c p) t -> p c t", p=128)
                        g.dma_start(out=BUF1[:, 4 * i:4 * i + 4, :], in_=srcp).then_inc(cin.sem, 16)
                    cin.n += 64
                sec("gpsimd", fg)
                emit_front(True, cin.n)
                emit_tots()

                def fv(v, b=b):
                    v.wait_ge(cv.sem, cv.n)
                    cv.inc(v.scalar_tensor_tensor(out=PRESUM[:], in0=TOTS[:],
                                                  scalar=MASKP[:, b:b + 1], in1=PRESUM[:],
                                                  op0=ALU.mult, op1=ALU.add))
                    v.wait_ge(cv.sem, cv.n)
                    cv.inc(v.scalar_tensor_tensor(out=POWN[:], in0=TOTS[:],
                                                  scalar=MASKP[:, 4 + b:5 + b], in1=POWN[:],
                                                  op0=ALU.mult, op1=ALU.add))
                sec("vector", fv)

            def fv(v):
                v.wait_ge(cv.sem, cv.n)
                cv.inc(v.tensor_copy(out=PKT[:, 0:32], in_=PRESUM[:]))
                cv.inc(v.memset(PKT[:, 32:33], MAGICF))
            sec("vector", fv)
            pkt_v = cv.n

            def fg(g):
                g.wait_ge(cv.sem, pkt_v)
                preg = REGS["p"]
                with g.If_eq(preg, one_bits):
                    g.dma_start(out=shx[1][:, 0:33], in_=PKT[:]).then_inc(cmb.sem, 16)
                with g.Else():
                    g.dma_start(out=shx[0][:, 0:33], in_=PKT[:]).then_inc(cmb.sem, 16)
                cmb.n += 16
                g.wait_ge(cmb.sem, cmb.n)
            sec("gpsimd", fg)

            def fg(g):
                g.wait_ge(cf.sem, cf.n)
                for i in range(4):
                    srcp = xT[512 * i:512 * (i + 1), :].rearrange("(c p) t -> p c t", p=128)
                    g.dma_start(out=BUF1[:, 4 * i:4 * i + 4, :], in_=srcp).then_inc(cin.sem, 16)
                cin.n += 64
            sec("gpsimd", fg)
            emit_front(False, cin.n, spill=True)

            def _fg_poll(g):
                preg = REGS["p"]
                raw1 = g.alloc_register("praw1")
                raw2 = g.alloc_register("praw2")
                ready = g.alloc_register("pready")
                g.reg_mov(ready, 0)
                NPOLL = 1536
                nd = 0
                for i in range(NPOLL):
                    reps = 1 if i < 256 else 8
                    with g.If_eq(ready, 0):
                        for _r in range(reps):
                            with g.If_eq(preg, one_bits):
                                g.dma_start(out=FLG[:, 0:1], in_=shx[0][0:1, 32:33]).then_inc(cpl.sem, 16)
                            with g.Else():
                                g.dma_start(out=FLG[:, 0:1], in_=shx[1][0:1, 32:33]).then_inc(cpl.sem, 16)
                            nd += 1
                            g.wait_ge(cpl.sem, 16 * nd)
                        g.load(raw1, FLG[0:1, 0:1].bitcast(mybir.dt.int32))
                        with g.If_eq(raw1, magic_bits):
                            g.reg_mov(ready, 1)
                with g.If_eq(ready, 0):
                    g.halt()
                with g.If_eq(preg, one_bits):
                    g.dma_start(out=RCV[:, 0:33], in_=shx[0][:, 0:33]).then_inc(cmb.sem, 16)
                with g.Else():
                    g.dma_start(out=RCV[:, 0:33], in_=shx[1][:, 0:33]).then_inc(cmb.sem, 16)
                cmb.n += 16
                g.wait_ge(cmb.sem, cmb.n)

            def emit_exchange():
                sec("gpsimd", _fg_poll)
                rcv_done = cmb.n

                def fv(v):
                    v.wait_ge(cmb.sem, rcv_done)
                    cv.inc(v.tensor_copy(out=RCV2[:, 0:33].bitcast(mybir.dt.int32),
                                         in_=RCV[:, 0:33].bitcast(mybir.dt.int32)))
                    v.wait_ge(cv.sem, cv.n)
                    cv.inc(v.tensor_tensor(out=CAR[:], in0=POWN[:],
                                           in1=RCV2[:, 0:32], op=ALU.add))
                sec("vector", fv)
                CARREF[0] = cv.n
            EXREF[0] = emit_exchange

        barrier("scalar", "vector", "tensor")

        def q1_evac(oc, n, bank_i, gv):
            ctn = ct.n
            def fs(s, bank_i=bank_i, gv=gv, ctn=ctn):
                s.wait_ge(cf.sem, gv)
                if ctn:
                    s.wait_ge(ct.sem, ctn)
                ca.inc(s.activation(out=TMP[:, bank_i % 2, :], in_=pb[bank_i][:],
                                    func=AF.Square))
            sec("scalar", fs)
            av = ca.n

            def ft(t, oc=oc, n=n, bank_i=bank_i, av=av):
                t.wait_ge(ca.sem, av)
                last = t.matmul(pb[4 + n][0:1, :], ONESC[:], TMP[:, bank_i % 2, :],
                                start=(oc == 0), stop=(oc == KC - 1))
                ct.inc(last)
            sec("tensor", ft)
            return (ct, ct.n)

        mm3_phase(wqkvT, 0, q1_evac, rhs_hi, rhs_lo, single_set=True)

        if kind == "F":
            EXREF[0]()

        barrier("vector")
        scan_done = []
        for fch in range(KC):
            def fv(v, fch=fch):
                w0 = BUF2[:, fch, 0:TH]
                w1 = BUF2[:, fch, TH:T]
                cv.inc(v.tensor_tensor_scan(out=w0, data0=w0, data1=w0,
                                            initial=CAR[:, fch:fch + 1],
                                            op0=ALU.add, op1=ALU.bypass))
                v.wait_ge(cv.sem, cv.n)
                cv.inc(v.tensor_tensor_scan(out=w1, data0=w1, data1=w1,
                                            initial=BUF2[:, fch, TH - 1:TH],
                                            op0=ALU.add, op1=ALU.bypass))
                kv0 = BUF3[:, fch, 0:TH]
                kv1 = BUF3[:, fch, TH:T]
                cv.inc(v.tensor_tensor_scan(out=kv0, data0=kv0, data1=kv0,
                                            initial=CAR[:, 16 + fch:17 + fch],
                                            op0=ALU.add, op1=ALU.bypass))
                v.wait_ge(cv.sem, cv.n)
                cv.inc(v.tensor_tensor_scan(out=kv1, data0=kv1, data1=kv1,
                                            initial=BUF3[:, fch, TH - 1:TH],
                                            op0=ALU.add, op1=ALU.bypass))
            sec("vector", fv)
            scan_done.append(cv.n)

        def fs_ln(s):
            s.wait_ge(ca.sem, ca.n)
            for fch in range(KC):
                s.wait_ge(cv.sem, scan_done[fch])
                for n in range(2):
                    wc = BUF2[:, fch, TH * n:TH * n + TH]
                    ca.inc(s.activation(out=wc, in_=wc, func=AF.Ln, bias=EPSB[:]))
        sec("scalar", fs_ln)
        ln_all = ca.n

        def fs_exp(s):
            s.wait_ge(ca.sem, ln_all)
            for fch in range(KC):
                for n in range(2):
                    wc = BUF2[:, fch, TH * n:TH * n + TH]
                    ca.inc(s.activation(out=wc, in_=wc, func=AF.Exp, scale=-1.0))
        sec("scalar", fs_exp)
        exp_all = ca.n

        def fv_yp(v):
            v.wait_ge(ca.sem, exp_all)
            for fch in range(KC):
                for n in range(2):
                    wc = BUF2[:, fch, TH * n:TH * n + TH]
                    kcm = BUF3[:, fch, TH * n:TH * n + TH]
                    cv.inc(v.tensor_tensor(out=kcm, in0=kcm, in1=wc, op=ALU.mult))
        sec("vector", fv_yp)
        ypre_all = cv.n

        barrier("scalar")
        def fs(s, can=ca.n):
            if can:
                s.wait_ge(ca.sem, can)
            s.wait_ge(ct.sem, ct.n)
            s.wait_ge(cv.sem, cv.n)
            for n in range(2):
                ca.inc(s.activation(out=R(n), in_=pb[4 + n][0:1, :],
                                    func=AF.Ln, scale=1.0 / D, bias=EPSC[0:1, 1:2]))
                s.wait_ge(ca.sem, ca.n)
                ca.inc(s.activation(out=R(n), in_=R(n), func=AF.Exp, scale=-0.5))
        sec("scalar", fs)

        bcq = bcast_rows((0, 1), (6, 7))
        tmpq_free = [(0, 0), (0, 0)]

        def q2_evac(oc, n, bank_i, gv):
            slot = bank_i % 2
            fa, fvv = tmpq_free[slot]

            def fv(v, bank_i=bank_i, gv=gv, n=n, slot=slot, fa=fa, fvv=fvv):
                v.wait_ge(cf.sem, gv)
                if fa:
                    v.wait_ge(ca.sem, fa)
                if fvv:
                    v.wait_ge(cv.sem, fvv)
                cv.inc(v.tensor_copy(out=TMP[:, slot, :], in_=pb[bank_i][:]))
                v.wait_ge(ct.sem, bcq)
                v.wait_ge(cv.sem, cv.n)
                cv.inc(v.tensor_tensor(out=TMP[:, slot, :], in0=TMP[:, slot, :],
                                       in1=pb[6 + n][:], op=ALU.mult))
            sec("vector", fv)
            vv = cv.n

            def fs(s, slot=slot, vv=vv):
                s.wait_ge(cv.sem, vv)
                ca.inc(s.activation(out=TMP[:, slot, :], in_=TMP[:, slot, :],
                                    func=AF.Sigmoid))
            sec("scalar", fs)
            av = ca.n

            def fv2(v, oc=oc, n=n, slot=slot, av=av):
                v.wait_ge(ca.sem, av)
                v.wait_ge(cv.sem, ypre_all)
                y32 = BUF3[:, oc, TH * n:TH * n + TH]
                cv.inc(v.tensor_tensor(out=y32, in0=y32, in1=TMP[:, slot, :],
                                       op=ALU.mult))
            sec("vector", fv2)
            vv2 = cv.n
            hi = B2b[:, oc, TH * n:TH * n + TH]
            lo = B2b[:, oc, T + TH * n:T + TH * n + TH]

            def fv3(v, oc=oc, n=n, hi=hi, lo=lo, vv2=vv2):
                v.wait_ge(cv.sem, vv2)
                cv.inc(v.tensor_copy(out=hi, in_=BUF3[:, oc, TH * n:TH * n + TH]))
                v.wait_ge(cv.sem, cv.n)
                cv.inc(v.tensor_tensor(out=lo, in0=BUF3[:, oc, TH * n:TH * n + TH],
                                       in1=hi, op=ALU.subtract))
            sec("vector", fv3)
            tmpq_free[slot] = (ca.n, cv.n)
            return (cv, cv.n)

        mm3_phase(wqkvT, 0, q2_evac, rhs_hi, rhs_lo, single_set=True)
        y_all = (cv.n, cf.n)

        barrier("scalar", "vector", "tensor")

        def yrhs_hi(kcg, n):
            return B2b[:, kcg, TH * n:TH * n + TH]

        def yrhs_lo(kcg, n):
            return B2b[:, kcg, T + TH * n:T + TH * n + TH]

        delta_init = [False] * (KC * 2)
        last_sevac2 = {}
        out_bank_free = {}

        for b_ff in range(8):
            for jl in range(8):
                j = 8 * b_ff + jl
                sis = []
                for s_ in range(8):
                    srcu = wswT[256 * s_:256 * s_ + 256, 128 * j:128 * j + 128] \
                        .rearrange("(kc p) c -> p kc c", p=128)
                    srcg = wswT[256 * s_:256 * s_ + 256, FFD + 128 * j:FFD + 128 * j + 128] \
                        .rearrange("(kc p) c -> p kc c", p=128)
                    sis.append(slab([(srcu, 0), (srcg, 256)]))

                sbase = 4 * (jl % 2)
                _obf = max([v for bk, (c_, v) in out_bank_free.items()
                            if sbase <= bk < sbase + 4] or [0])

                def ft(t, sis=tuple(sis), lse=last_sevac2.get(jl % 2, (0, 0)),
                       sbase=sbase, _obf=_obf):
                    if lse[0]:
                        t.wait_ge(ca.sem, lse[0])
                    if lse[1]:
                        t.wait_ge(cv.sem, lse[1])
                    if _obf:
                        t.wait_ge(cv.sem, _obf)
                    last = None
                    for s_, si in enumerate(sis):
                        wait_slab(t, si)
                        b = si % NBUF
                        for kcl in range(2):
                            kcg = 2 * s_ + kcl
                            for un in range(2):
                                o0 = 256 * un + 128 * kcl
                                whi = WHLb[:, b, o0:o0 + 128]
                                wlo = WHLb[:, b, 512 + o0:512 + o0 + 128]
                                for n in range(2):
                                    bank = pb[sbase + 2 * un + n]
                                    t.matmul(bank[:], whi, yrhs_hi(kcg, n),
                                             start=(kcg == 0), stop=False)
                                    t.matmul(bank[:], whi, yrhs_lo(kcg, n),
                                             start=False, stop=False)
                                    last = t.matmul(bank[:], wlo, yrhs_hi(kcg, n),
                                                    start=False, stop=(kcg == KC - 1))
                        last.then_inc(cf.sem, 1)
                        cf.n += 1
                sec("tensor", ft)
                gv = cf.n

                for n in range(2):
                    def fs(s, n=n, gv=gv, sbase=sbase):
                        s.wait_ge(cf.sem, gv)
                        ca.inc(s.activation(out=TMP[:, n, :], in_=pb[sbase + 2 + n][:],
                                            func=AF.Silu))
                    sec("scalar", fs)
                    av = ca.n

                    def fv(v, n=n, av=av, sbase=sbase):
                        v.wait_ge(ca.sem, av)
                        cv.inc(v.tensor_tensor(out=TMP[:, n, :], in0=pb[sbase + n][:],
                                               in1=TMP[:, n, :], op=ALU.mult))
                    sec("vector", fv)
                    shi = B3b[:, jl, TH * n:TH * n + TH]
                    slo = B3b[:, jl, T + TH * n:T + TH * n + TH]

                    def fv2(v, shi=shi, slo=slo, n=n):
                        v.wait_ge(cv.sem, cv.n)
                        cv.inc(v.tensor_copy(out=shi, in_=TMP[:, n, :]))
                        v.wait_ge(cv.sem, cv.n)
                        cv.inc(v.tensor_tensor(out=slo, in0=TMP[:, n, :], in1=shi,
                                               op=ALU.subtract))
                    sec("vector", fv2)
                last_sevac2[jl % 2] = (ca.n, cv.n)
            s_block_done = cv.n

            for og in range(8):
                sis = []
                for ks in range(4):
                    src = woutT[1024 * b_ff + 256 * ks:1024 * b_ff + 256 * ks + 256,
                                256 * og:256 * og + 256] \
                        .rearrange("(kc p) c -> p kc c", p=128)
                    sis.append(slab([(src, 0)]))

                obase = 4 * (og % 2)

                def ft(t, sis=tuple(sis), sbd=s_block_done, obf=dict(out_bank_free),
                       obase=obase):
                    t.wait_ge(cv.sem, sbd)
                    for bi in range(obase, obase + 4):
                        if bi in obf:
                            cc_, val = obf[bi]
                            t.wait_ge(cc_.sem, val)
                    last = None
                    for ks, si in enumerate(sis):
                        wait_slab(t, si)
                        b = si % NBUF
                        for kcl in range(2):
                            kl = 2 * ks + kcl
                            for mc in range(2):
                                o0 = 256 * kcl + 128 * mc
                                whi = WHLb[:, b, o0:o0 + 128]
                                wlo = WHLb[:, b, 512 + o0:512 + o0 + 128]
                                for n in range(2):
                                    bank = pb[obase + 2 * mc + n]
                                    shi = B3b[:, kl, TH * n:TH * n + TH]
                                    slo = B3b[:, kl, T + TH * n:T + TH * n + TH]
                                    t.matmul(bank[:], whi, shi, start=(kl == 0), stop=False)
                                    t.matmul(bank[:], whi, slo, start=False, stop=False)
                                    last = t.matmul(bank[:], wlo, shi, start=False,
                                                    stop=(kl == 7))
                        last.then_inc(cf.sem, 1)
                        cf.n += 1
                sec("tensor", ft)
                gv = cf.n

                for mc in range(2):
                    for n in range(2):
                        oc = 2 * og + mc
                        dst = BUF1[:, oc, TH * n:TH * n + TH]
                        idx = 2 * oc + n

                        def fv(v, dst=dst, mc=mc, n=n, gv=gv, idx=idx, obase=obase):
                            v.wait_ge(cf.sem, gv)
                            if delta_init[idx]:
                                cv.inc(v.tensor_tensor(out=dst, in0=dst,
                                                       in1=pb[obase + 2 * mc + n][:],
                                                       op=ALU.add))
                            else:
                                cv.inc(v.tensor_copy(out=dst, in_=pb[obase + 2 * mc + n][:]))
                        sec("vector", fv)
                        out_bank_free[obase + 2 * mc + n] = (cv, cv.n)
                        delta_init[idx] = True
        delta_all = cv.n

        def fg(g):
            g.wait_ge(cf.sem, cf.n)
            for i in range(4):
                srcp = x2pd[512 * i:512 * (i + 1), :].rearrange("(c p) t -> p c t", p=128)
                g.dma_start(out=BUF2[:, 4 * i:4 * i + 4, :], in_=srcp).then_inc(cin.sem, 16)
            cin.n += 64
        sec("gpsimd", fg)
        x_re = cin.n

        barrier("vector")
        for n in range(2):
            for fch in range(KC):
                xs = BUF2[:, fch, TH * n:TH * n + TH]
                dl = BUF1[:, fch, TH * n:TH * n + TH]

                def fv(v, xs=xs, dl=dl, fch=fch):
                    v.wait_ge(cin.sem, x_re)
                    cv.inc(v.scalar_tensor_tensor(out=xs, in0=dl,
                                                  scalar=ga_c[:, fch:fch + 1], in1=xs,
                                                  op0=ALU.mult, op1=ALU.add))
                sec("vector", fv)

        barrier("scalar", "vector", "tensor")
        ln_stats(BUF2[:], want_mean=True)

        def h2_apply(fch, n, slot, vv):
            sl = 8 + fch // 4
            c0 = (fch % 4) * T + TH * n
            h28 = B3f8[:, sl, c0:c0 + TH]

            def fv(v, h28=h28, fch=fch, slot=slot, vv=vv):
                v.wait_ge(cv.sem, vv)
                cv.inc(v.tensor_scalar(out=h28, in0=TMP[:, slot, :],
                                       scalar1=cb1_c[:, fch:fch + 1],
                                       scalar2=sb_c[:, fch:fch + 1],
                                       op0=ALU.mult, op1=ALU.add))
            sec("vector", fv)
            return (0, cv.n)

        h_like_loop(BUF2[:], h2_apply)
        h2_all = ca.n

        def h2rhs(kcg, n):
            sl = 8 + kcg // 4
            c0 = (kcg % 4) * T + TH * n
            return B3f8[:, sl, c0:c0 + TH]

        barrier("scalar", "vector", "tensor")
        o_init = [False] * (KC * 2)
        last_fevac2 = {}
        lin_bank_free = {}

        for b_ff in range(8):
            for jl in range(8):
                j = 8 * b_ff + jl
                sis = []
                for gate in range(2):
                    for ks in range(4):
                        src = wg[gate, 512 * ks:512 * ks + 512, 128 * j:128 * j + 128] \
                            .rearrange("(kc p) c -> p kc c", p=128)
                        sis.append(slab([(src, 0)], fp8=True))

                fbase = 4 * (jl % 2)
                _lbf = max([v for bk, (c_, v) in lin_bank_free.items()
                            if fbase <= bk < fbase + 4] or [0])

                def ft(t, sis=tuple(sis), lfe=last_fevac2.get(jl % 2, (0, 0)),
                       fbase=fbase, _lbf=_lbf):
                    if lfe[0]:
                        t.wait_ge(ca.sem, lfe[0])
                    if lfe[1]:
                        t.wait_ge(cv.sem, lfe[1])
                    if _lbf:
                        t.wait_ge(cv.sem, _lbf)
                    last = None
                    for idx, si in enumerate(sis):
                        gate, ks = divmod(idx, 4)
                        wait_slab(t, si)
                        b = si % NBUF
                        wq8 = WHLf8[:, b, 0:512].rearrange("p (kc c) -> p kc c", kc=4)
                        for kcl in range(4):
                            kcg = 4 * ks + kcl
                            for n in range(2):
                                bank = pb[fbase + 2 * gate + n]
                                last = t.matmul(bank[:], wq8[:, kcl, :], h2rhs(kcg, n),
                                                start=(kcg == 0), stop=(kcg == KC - 1))
                        last.then_inc(cf.sem, 1)
                        cf.n += 1
                sec("tensor", ft)
                gv = cf.n

                for n in range(2):
                    f0b = TMPb[:, 0, 0:TH]
                    f1b = TMPb[:, 0, TH:2 * TH]
                    actb = TMPb[:, 1, 0:TH]
                    a8 = B3f8[:, 12 + jl // 4, (jl % 4) * T + TH * n:(jl % 4) * T + TH * n + TH]
                    lfe = last_fevac2.get(2, (0, 0))

                    def fv0(v, n=n, gv=gv, f0b=f0b, f1b=f1b, lfe=lfe, fbase=fbase):
                        v.wait_ge(cf.sem, gv)
                        if lfe[0]:
                            v.wait_ge(ca.sem, lfe[0])
                        if lfe[1]:
                            v.wait_ge(cv.sem, lfe[1])
                        cv.inc(v.tensor_copy(out=f0b, in_=pb[fbase + n][:]))
                        v.wait_ge(cv.sem, cv.n)
                        cv.inc(v.tensor_copy(out=f1b, in_=pb[fbase + 2 + n][:]))
                    sec("vector", fv0)
                    vv0 = cv.n

                    def fs(s, f0b=f0b, vv0=vv0):
                        s.wait_ge(cv.sem, vv0)
                        ca.inc(s.activation(out=f0b, in_=f0b, func=AF.Gelu_apprx_tanh))
                    sec("scalar", fs)
                    av = ca.n

                    def fv(v, f0b=f0b, f1b=f1b, actb=actb, a8=a8, av=av):
                        v.wait_ge(ca.sem, av)
                        cv.inc(v.tensor_tensor(out=actb, in0=f0b, in1=f1b, op=ALU.mult))
                        v.wait_ge(cv.sem, cv.n)
                        cv.inc(v.tensor_copy(out=a8, in_=actb))
                    sec("vector", fv)
                    last_fevac2[2] = (ca.n, cv.n)
                last_fevac2[jl % 2] = (ca.n, cv.n)
            act8_done = (ca.n, cv.n)

            for og in range(8):
                sis = []
                for ks in range(4):
                    src = wlin[1024 * b_ff + 256 * ks:1024 * b_ff + 256 * ks + 256,
                               256 * og:256 * og + 256] \
                        .rearrange("(kc p) c -> p kc c", p=128)
                    sis.append(slab([(src, 0)], fp8=True))

                obase = 4 * (og % 2)
                _gf = last_fevac2.get(og % 2, (0, 0))

                def ft(t, sis=tuple(sis), ad=act8_done, lbf=dict(lin_bank_free),
                       obase=obase, _gf=_gf):
                    t.wait_ge(ca.sem, ad[0])
                    t.wait_ge(cv.sem, ad[1])
                    if _gf[0]:
                        t.wait_ge(ca.sem, _gf[0])
                    if _gf[1]:
                        t.wait_ge(cv.sem, _gf[1])
                    for bi in range(obase, obase + 4):
                        if bi in lbf:
                            cc_, val = lbf[bi]
                            t.wait_ge(cc_.sem, val)
                    last = None
                    for ks, si in enumerate(sis):
                        wait_slab(t, si)
                        b = si % NBUF
                        wq8 = WHLf8[:, b, 0:512].rearrange("p (kc c) -> p kc c", kc=2)
                        for kcl in range(2):
                            kl = 2 * ks + kcl
                            for mc in range(2):
                                for n in range(2):
                                    bank = pb[obase + 2 * mc + n]
                                    a8s = B3f8[:, 12 + kl // 4,
                                               (kl % 4) * T + TH * n:(kl % 4) * T + TH * n + TH]
                                    last = t.matmul(bank[:], wq8[:, kcl, 128 * mc:128 * mc + 128],
                                                    a8s, start=(kl == 0), stop=(kl == 7))
                        last.then_inc(cf.sem, 1)
                        cf.n += 1
                sec("tensor", ft)
                gv = cf.n

                for mc in range(2):
                    for n in range(2):
                        oc = 2 * og + mc
                        dst = BUF1[:, oc, TH * n:TH * n + TH]
                        idx = 2 * oc + n

                        def fv(v, dst=dst, mc=mc, n=n, gv=gv, idx=idx, obase=obase):
                            v.wait_ge(cf.sem, gv)
                            if o_init[idx]:
                                cv.inc(v.tensor_tensor(out=dst, in0=dst,
                                                       in1=pb[obase + 2 * mc + n][:],
                                                       op=ALU.add))
                            else:
                                cv.inc(v.tensor_copy(out=dst, in_=pb[obase + 2 * mc + n][:]))
                        sec("vector", fv)
                        lin_bank_free[obase + 2 * mc + n] = (cv, cv.n)
                        o_init[idx] = True

        barrier("scalar", "vector")
        stage_ct = cf.n
        store_done = {}
        prev_obf = 0
        u = 0
        for fch in range(KC):
            for n in range(2):
                osrc = BUF1[:, fch, TH * n:TH * n + TH]
                x2s = BUF2[:, fch, TH * n:TH * n + TH]
                obf = TMPb[:, 0, 0:TH]
                stg = BUF3[:, 14 + (u % 2), 0:TH]
                sd = store_done.get(u - 2, 0)

                def fv_c(v, osrc=osrc, obf=obf, pv=prev_obf):
                    if pv:
                        v.wait_ge(cv.sem, pv)
                    cv.inc(v.tensor_copy(out=obf, in_=osrc))
                sec("vector", fv_c)
                av = cv.n

                def fv(v, obf=obf, x2s=x2s, stg=stg, fch=fch, av=av, sd=sd, u=u):
                    v.wait_ge(cv.sem, av)
                    if u < 2:
                        v.wait_ge(cf.sem, stage_ct)
                    if sd:
                        v.wait_ge(co.sem, sd)
                    cv.inc(v.scalar_tensor_tensor(out=stg, in0=obf,
                                                  scalar=gb_c[:, fch:fch + 1], in1=x2s,
                                                  op0=ALU.mult, op1=ALU.add))
                sec("vector", fv)
                prev_obf = cv.n
                vv = cv.n

                def fg(g, fch=fch, n=n, stg=stg, vv=vv):
                    g.wait_ge(cv.sem, vv)
                    if co.n:
                        g.wait_ge(co.sem, co.n)
                    g.dma_start(out=out[128 * fch:128 * fch + 128, TH * n:TH * n + TH],
                                in_=stg).then_inc(co.sem, 16)
                    co.n += 16
                sec("sync", fg)
                store_done[u] = co.n
                u += 1

        def fg(g):
            g.wait_ge(co.sem, co.n)
        sec("sync", fg)

    return nc


_NC_CACHE = {}
FUSED = True


def kernel(L0_attn_x, L0_attn_w_qkv, L0_attn_w_swiglu, L0_attn_w_out,
           L0_ffn_w_gating, L0_ffn_w_linear, L0_c, L0_wmod):
    x = np.ascontiguousarray(np.asarray(L0_attn_x, dtype=np.float32).reshape(8192, 2048).T)
    wqkvT = np.ascontiguousarray(np.asarray(L0_attn_w_qkv, dtype=np.float32).T)
    wswT = np.ascontiguousarray(np.asarray(L0_attn_w_swiglu, dtype=np.float32).T)
    woutT = np.ascontiguousarray(np.asarray(L0_attn_w_out, dtype=np.float32).T)
    wg = np.ascontiguousarray(np.asarray(L0_ffn_w_gating, dtype=np.float32))
    wlin = np.ascontiguousarray(np.asarray(L0_ffn_w_linear, dtype=np.float32))
    cT = np.ascontiguousarray(np.asarray(L0_c, dtype=np.float32).reshape(16, 128).T)
    wmod = np.ascontiguousarray(np.asarray(L0_wmod, dtype=np.float32))

    xsh = [np.ascontiguousarray(x[:, 1024 * i:1024 * (i + 1)]) for i in range(NCORES)]

    if FUSED:
        xpres, maskps, msibs = [], [], []
        for cid in range(NCORES):
            pidx = cid // 2
            if cid % 2 == 0:
                globs = list(range(0, 2 * pidx + 1, 2))
            else:
                globs = list(range(1, 2 * pidx, 2))
            cols = [x[:, 1024 * g:1024 * (g + 1)] for g in globs]
            cols += [np.zeros((2048, 1024), np.float32)] * (4 - len(cols))
            xpres.append(np.ascontiguousarray(np.concatenate(cols, axis=1)))
            mp = np.zeros((128, 8), np.float32)
            mp[:, :len(globs)] = 1.0
            nown = len(globs) - (1 if cid % 2 == 0 and globs else 0)
            mp[:, 4:4 + nown] = 1.0
            maskps.append(np.ascontiguousarray(mp))
            msibs.append(np.full((128, 1), float(cid % 2), np.float32))
        if "F" not in _NC_CACHE:
            _NC_CACHE["F"] = build("F")
        mapsF = [{"xT": xsh[i], "xpre": xpres[i], "maskp": maskps[i],
                  "msib": msibs[i], "wqkvT": wqkvT, "wswT": wswT, "woutT": woutT,
                  "wg": wg, "wlin": wlin, "cT": cT, "wmod": wmod}
                 for i in range(NCORES)]
        resF = run_bass_kernel_spmd(_NC_CACHE["F"], mapsF, core_ids=list(range(NCORES)))
        outs = [np.asarray(resF.results[i]["out"]) for i in range(NCORES)]
        full = np.concatenate([o.T for o in outs], axis=0)
        return full.reshape(1, 8192, 2048)

    if "A" not in _NC_CACHE:
        _NC_CACHE["A"] = build("A")
    mapsA = [{"xT": xsh[i], "wqkvT": wqkvT, "cT": cT, "wmod": wmod}
             for i in range(NCORES)]
    resA = run_bass_kernel_spmd(_NC_CACHE["A"], mapsA, core_ids=list(range(NCORES)))
    tot8 = np.ascontiguousarray(
        np.stack([np.asarray(resA.results[i]["tot"]) for i in range(NCORES)], axis=0))

    masks = [np.ascontiguousarray(
        np.tile((np.arange(8) < i).astype(np.float32), (128, 1)))
        for i in range(NCORES)]

    if "B" not in _NC_CACHE:
        _NC_CACHE["B"] = build("B")
    mapsB = [{"xT": xsh[i], "wqkvT": wqkvT, "wswT": wswT, "woutT": woutT,
              "wg": wg, "wlin": wlin, "cT": cT, "wmod": wmod,
              "tot8": tot8, "mask8": masks[i]} for i in range(NCORES)]
    resB = run_bass_kernel_spmd(_NC_CACHE["B"], mapsB, core_ids=list(range(NCORES)))

    outs = [np.asarray(resB.results[i]["out"]) for i in range(NCORES)]
    full = np.concatenate([o.T for o in outs], axis=0)
    return full.reshape(1, 8192, 2048)
